# version 18
# baseline (speedup 1.0000x reference)
import numpy as np
import ml_dtypes
import concourse.bass as bass
import concourse.mybir as mybir
from concourse.bass_utils import run_bass_kernel_spmd

F32 = mybir.dt.float32
BF16 = mybir.dt.bfloat16
AF = mybir.ActivationFunctionType
ALU = mybir.AluOpType
AX = mybir.AxisListType.X

S = 8192
TO = 4096
NE = 8
NI = 1408
NIT = 11
EPS = 1e-5


class Ctr:
    def __init__(self, handle):
        self.h = handle
        self.v = 0


class Q:

    def __init__(self, lanes):
        self.lanes = lanes
        self.i = 0

    def issue(self):
        lane = self.lanes[self.i % len(self.lanes)]
        self.i += 1
        lane.v += 16
        return (lane, lane.v)

    def barrier(self):
        return [(ln, ln.v) for ln in self.lanes if ln.v > 0]


class Emit:

    def __init__(self, sems):
        self.q = {k: [] for k in ("t", "v", "s", "y", "g")}
        self.selfsem = {"v": sems["vc"], "s": sems["sc"],
                        "g": sems.get("gc")}
        self.phase = "init" 

    def op(self, eng, fn, wait=(), inc=(), dma=False):
        waits = []
        for w in wait:
            if w is None:
                continue
            if isinstance(w, list):
                waits.extend(w)
            else:
                waits.append(w)
        waits = [(c, val) for (c, val) in waits if val > 0]
        chain = self.selfsem.get(eng)
        if chain is not None and not dma and chain.v > 0:
            waits.append((chain, chain.v))
        deltas = []
        for (c, n) in inc:
            c.v += n
            deltas.append((c, n))
        if (chain is not None and not dma
                and chain not in [c for (c, n) in deltas]):
            chain.v += 1
            deltas.append((chain, 1))

        def run(e):
            best = {}
            for (c, val) in waits:
                k = id(c)
                if k not in best or best[k][1] < val:
                    best[k] = (c, val)
            for (c, val) in best.values():
                e.wait_ge(c.h, val)
            ins = fn(e)
            if ins is not None:
                for (c, n) in deltas:
                    ins = ins.then_inc(c.h, n)
            return ins

        self.q[eng].append((self.phase, run))

    def dma(self, eng, q, fn, wait=()):
        tok = q.issue()
        self.op(eng, fn, wait=wait, inc=[tok[:1][0:0]] or [], dma=True)
        raise RuntimeError("use dma2")

    def dma2(self, eng, q, fn, wait=()):
        lane = q.lanes[q.i % len(q.lanes)]
        q.i += 1
        prev = (lane, lane.v) if lane.v > 0 else None
        lane.v += 16
        tok = (lane, lane.v)
        waits = []
        for w in wait:
            if w is None:
                continue
            if isinstance(w, list):
                waits.extend(w)
            else:
                waits.append(w)
        waits = [(c, val) for (c, val) in waits if val > 0]
        if prev is not None:
            waits.append(prev)

        def run(e):
            best = {}
            for (c, val) in waits:
                k = id(c)
                if k not in best or best[k][1] < val:
                    best[k] = (c, val)
            for (c, val) in best.values():
                e.wait_ge(c.h, val)
            ins = fn(e)
            ins.then_inc(lane.h, 16)
            return ins

        self.q[eng].append((self.phase, run))
        return tok


def build_program(replica_groups):
    nc = bass.Bass("TRN2", target_bir_lowering=False)

    xT = nc.dram_tensor("xT", [8, 128, S], F32, kind="ExternalInput").ap()
    WT = nc.dram_tensor("WTm", [8, 128, 3072], F32, kind="ExternalInput").ap()
    bqk = nc.dram_tensor("bqk", [128, 16], F32, kind="ExternalInput").ap()
    bvb = nc.dram_tensor("bvb", [128, 1024], BF16, kind="ExternalInput").ap()
    rT = nc.dram_tensor("rT", [128, 64], BF16, kind="ExternalInput").ap()
    wg = nc.dram_tensor("wg", [NE, 8, 128, NI], F32, kind="ExternalInput").ap()
    wu = nc.dram_tensor("wu", [NE, 8, 128, NI], F32, kind="ExternalInput").ap()
    wd = nc.dram_tensor("wd", [NE, NIT, 128, 1024], F32,
                        kind="ExternalInput").ap()
    wgs = nc.dram_tensor("wgs", [8, 128, NI], F32, kind="ExternalInput").ap()
    wus = nc.dram_tensor("wus", [8, 128, NI], F32, kind="ExternalInput").ap()
    wds = nc.dram_tensor("wds", [NIT, 128, 1024], F32,
                         kind="ExternalInput").ap()
    ones_in = nc.dram_tensor("ones_in", [128, 128], BF16,
                             kind="ExternalInput").ap()
    ident_in = nc.dram_tensor("ident_in", [128, 128], F32,
                              kind="ExternalInput").ap()
    sel_in = nc.dram_tensor("sel_in", [8, 1024], BF16,
                            kind="ExternalInput").ap()
    flag_in = nc.dram_tensor("flag_in", [128, 128], F32,
                             kind="ExternalInput").ap()
    eps_in = nc.dram_tensor("eps_in", [128, 1], F32, kind="ExternalInput").ap()
    onesfr_in = nc.dram_tensor("onesfr_in", [128, 128], mybir.dt.float32r,
                               kind="ExternalInput").ap()
    out = nc.dram_tensor("out", [8, 128, TO], F32, kind="ExternalOutput").ap()

    ln1_s = nc.dram_tensor("ln1_s", [8, 128, S], BF16).ap()
    kT_s = nc.dram_tensor("kT_s", [8, 128, S], BF16).ap()
    v_s = nc.dram_tensor("v_s", [64, 128, 1024], BF16).ap()
    E_s = nc.dram_tensor("E_s", [8, 64, 128, 512], BF16).ap()
    x2_s = nc.dram_tensor("x2_s", [8, 128, TO], F32).ap()
    hp_s = nc.dram_tensor("hp_s", [9, NIT, 128, TO], BF16).ap()
    wdb_s = nc.dram_tensor("wdb_s", [9, NIT, 128, 1024], BF16).ap()
    wgb_s = nc.dram_tensor("wgb_s", [9, 8, 128, NI], BF16).ap()
    wub_s = nc.dram_tensor("wub_s", [9, 8, 128, NI], BF16).ap()
    cc_in = nc.dram_tensor("cc_in", [128, 128], F32).ap()
    cc_out = nc.dram_tensor("cc_out", [256, 128], F32).ap()

    from contextlib import ExitStack
    es = ExitStack()
    with es:
        T1 = es.enter_context(nc.sbuf_tensor("T1", [128, 32768], BF16))
        T2 = es.enter_context(nc.sbuf_tensor("T2", [128, 32768], BF16))
        T3 = es.enter_context(nc.sbuf_tensor("T3", [128, 24576], BF16))
        T4 = es.enter_context(nc.sbuf_tensor("T4", [128, 15872], BF16))
        ps = es.enter_context(nc.psum_tensor("ps", [128, 4096], F32))

        F32R = mybir.dt.float32r

        def bank(i, w=512):
            return ps[:, i * 512: i * 512 + w]

        def b16(arena, off, n):
            return arena[:, off: off + n]

        def f32(arena, off, n):
            return arena[:].bitcast(F32)[:, off: off + n]

        ones_sb = b16(T4, 0, 128)
        onesfr_t = es.enter_context(nc.sbuf_tensor(
            "onesfr", [128, 128], mybir.dt.float32r))
        onesfr_sb = onesfr_t[:]
        ident_sb = f32(T4, 64, 128)
        flag_sb = f32(T4, 192, 128)
        bqk_sb = f32(T4, 320, 16)
        bvb_sb = b16(T4, 672, 1024)
        rT_sb = b16(T4, 1696, 64)
        m_all = f32(T4, 880, 64)
        z_all = f32(T4, 944, 64)
        zparts = f32(T4, 1008, 4)
        mtmp = f32(T4, 1012, 4)
        s0_sb = f32(T4, 1024, 128)
        s1_sb = f32(T4, 1152, 128)
        fj = f32(T4, 1280, 64)
        cm = f32(T4, 1344, 320)
        lg_sb = f32(T4, 1664, 256)
        cw_sb = f32(T4, 1920, 256)
        rtmp = f32(T4, 2176, 1088)
        eps_sb = f32(T4, 3263, 1)
        SEL = T4[0:8, 6528:7552]
        CWT = T4[0:8, 7552:11648]
        xbf2 = [b16(T4, 11648 + i * 1024, 1024) for i in range(2)]
        sq2 = [b16(T4, 13696 + i * 1024, 1024) for i in range(2)]
        d1_gs = [b16(T4, 11648, 2048), b16(T4, 13696, 2048)]

        a_xbf = [b16(T1, i * 1024, 1024) for i in range(8)]
        a_sq = [b16(T1, 8192 + i * 1024, 1024) for i in range(8)]
        a_xin = [f32(T2, 12288 + i * 2048, 2048) for i in range(2)]
        a_ln1 = [b16(T4, 6528, 2048), b16(T4, 8576, 2048),
                 b16(T4, 11648, 2048), b16(T4, 13696, 2048)]
        qT = T1[:]
        vpA = T1[:]
        xf = T1[:]

        mbb_sb = b16(T2, 0, 8192)
        isbb_sb = b16(T2, 8192, 8192)
        a3_ln1c = [b16(T2, 16384 + i * 4096, 4096) for i in range(2)]
        a_vs = [f32(T1, 8192 + i * 1024, 1024) for i in range(4)]
        a_mbf = [f32(T1, 12288 + i * 1024, 1024) for i in range(4)]
        a3_kbuf = [b16(T2, 8192 + i * 512, 512) for i in range(2)]
        a3_vbuf = [b16(T2, 9216 + i * 1024, 1024) for i in range(2)]
        eb = [b16(T2, i * 2048, 2048) for i in range(2)]
        kt_ring = [b16(T2, 8192 + i * 2048, 2048) for i in range(2)]
        vpB = T2[:]
        mb2_sb = f32(T2, 0, 4096)
        isb2_sb = f32(T2, 4096, 4096)
        mbb2_sb = b16(T2, 16384, 4096)
        isbb2_sb = b16(T2, 20480, 4096)
        wgu_cur = [b16(T2, 0, 11264), b16(T2, 11264, 11264)]
        cwb_sb = f32(T2, 11264, 4096)
        d2_hp = [b16(T2, 30720 + i * 512, 512) for i in range(4)]

        WT_sb = b16(T3, 0, 24576)
        e_ring = [b16(T3, i * 512, 512) for i in range(8)]
        cx_inb = [f32(T3, 2048, 4096), f32(T3, 2048, 4096)]
        cx2 = f32(T3, 6144, 4096)
        xin2 = [f32(T3, i * 2048, 2048) for i in range(2)]
        wgu_nxt = [b16(T3, 0, 11264), b16(T3, 11264, 11264)]
        d1_up = [b16(T3, 22528, 2048), b16(T4, 4352, 2048)]
        d2_wd = [b16(T3, i * 1024, 1024) for i in range(4)]
        d2_x2 = f32(T3, 2048, 4096)
        d2_x3 = f32(T3, 6144, 4096)

        sems = {}
        for name in ("tc", "vc", "sc", "cc", "gc"):
            sems[name] = Ctr(es.enter_context(nc.semaphore(name)))
        tc = sems["tc"]; vc = sems["vc"]; sc = sems["sc"]; cc = sems["cc"]
        gc = sems["gc"]

        def mk_q(pfx, n=8):
            return Q([Ctr(es.enter_context(nc.semaphore(f"{pfx}{i}")))
                      for i in range(n)])

        qA = mk_q("qA")
        qC = mk_q("qC")
        qO = mk_q("qO")
        qB = mk_q("qB")
        qW = mk_q("qW")
        qH = mk_q("qH", 12)
        qG = mk_q("qG", 1)
        qK = mk_q("qK")
        qO2 = mk_q("qO2")

        em = Emit(sems)
        E = em.op
        D = em.dma2

        k_consts = []
        k_consts.append(D("y", qC, lambda e: e.dma_start(out=ones_sb,
                                                         in_=ones_in)))
        k_consts.append(D("y", qC, lambda e: e.dma_start(out=ident_sb,
                                                         in_=ident_in)))
        k_consts.append(D("y", qC, lambda e: e.dma_start(out=flag_sb,
                                                         in_=flag_in)))
        k_consts.append(D("y", qC, lambda e: e.dma_start(out=bqk_sb,
                                                         in_=bqk)))
        k_consts.append(D("y", qC, lambda e: e.dma_start(out=bvb_sb,
                                                         in_=bvb)))
        k_consts.append(D("y", qC, lambda e: e.dma_start(out=rT_sb, in_=rT)))
        k_consts.append(D("y", qC, lambda e: e.dma_start(out=eps_sb,
                                                         in_=eps_in)))
        k_consts.append(D("y", qC, lambda e: e.dma_start(out=onesfr_sb,
                                                         in_=onesfr_in)))
        wt_tok = D("g", qG, lambda e: e.dma_start(
            out=WT_sb.rearrange("p (dt f) -> p dt f", dt=8),
            in_=WT.rearrange("dt p f -> p dt f")))

        em.phase = "a1"
        tmark = {}
        vmark = {}
        smark = {}
        F32R = mybir.dt.float32r
        t1addr = nc.lookup_mloc("T1").addr
        a1x = [nc.alloc_sbuf_tensor_at(
                   f"a1x{i}", [128, 4096], F32R,
                   offset=t1addr + i * 16384).ap() for i in range(2)]
        a1x_all = nc.alloc_sbuf_tensor_at(
            "a1xall", [128, 8192], F32R, offset=t1addr).ap()
        a1sq = [nc.alloc_sbuf_tensor_at(
                    f"a1sq{i}", [128, 1024], F32R,
                    offset=t1addr + 32768 + i * 4096).ap() for i in range(2)]
        a1x_f = [nc.alloc_sbuf_tensor_at(
                     f"a1xf{i}", [128, 4096], F32,
                     offset=t1addr + i * 16384).ap() for i in range(2)]
        a1sq_f = [nc.alloc_sbuf_tensor_at(
                      f"a1sqf{i}", [128, 1024], F32,
                      offset=t1addr + 32768 + i * 4096).ap() for i in range(2)]
        a_vs = [f32(T1, 10240 + i * 1024, 1024) for i in range(2)]
        a_mbf = [f32(T1, 12288 + i * 1024, 1024) for i in range(2)]
        omark = {}
        a2x = [b16(T2, 24576 + i * 2048, 2048) for i in range(4)]

        def emit_a2_chunk(tcn):
            c0 = tcn * 2048
            for dt in range(8):
                g = tcn * 8 + dt
                xbt = a2x[g % 4]
                lo = a_ln1[g % 4]
                tok = D("g", qW, lambda e, dt=dt, c0=c0, xbt=xbt:
                    e.dma_start(out=xbt, in_=xT[dt, :, c0:c0 + 2048]),
                    wait=[vmark.get(("a2m", g - 4))])
                E("v", lambda e, xbt=xbt, lo=lo, c0=c0: e.tensor_tensor(
                    lo, xbt, mbb_sb[:, c0:c0 + 2048], ALU.subtract),
                    wait=[tok, smark[("a1i", tcn * 2)],
                          smark[("a1i", tcn * 2 + 1)],
                          omark.get(("a2", g - 4))])
                vmark[("a2m", g)] = (vc, vc.v)
                E("v", lambda e, lo=lo, c0=c0: e.tensor_tensor(
                    lo, lo, isbb_sb[:, c0:c0 + 2048], ALU.mult))
                omark[("a2", g)] = D("s", qO2, lambda e, dt=dt, c0=c0, lo=lo:
                    e.dma_start(out=ln1_s[dt, :, c0:c0 + 2048], in_=lo),
                    wait=[(vc, vc.v)])

        for scn in range(8):
            par = scn % 2
            c0 = scn * 1024
            for dg in range(2):
                gb = scn * 2 + dg
                xbuf = a1x[dg]
                ltoks = []
                for hf in range(2):
                    ltoks.append(D("y" if dg == 0 else "s",
                        qA if dg == 0 else qB,
                        lambda e, c0=c0, dg=dg, xbuf=xbuf, hf=hf:
                        e.dma_start(
                            out=xbuf[:, hf * 2048:hf * 2048 + 2048]
                            .rearrange("p (dt c) -> p dt c", dt=2),
                            in_=xT[dg * 4 + hf * 2:dg * 4 + hf * 2 + 2,
                                   :, c0:c0 + 1024]
                            .rearrange("dt p c -> p dt c").bitcast(F32R)),
                        wait=[tmark.get(("a1", gb - 2)),
                              vmark.get(("a1sq", gb - 2))]))
                for k in range(4):
                    dt = dg * 4 + k
                    ltok = ltoks[k // 2]
                    xsl = xbuf[:, k * 1024:k * 1024 + 1024]
                    xslf = a1x_f[dg][:, k * 1024:k * 1024 + 1024]
                    sq = a1sq[dt % 2]
                    sqf = a1sq_f[dt % 2]
                    E("s", lambda e, xslf=xslf, sqf=sqf: e.activation(
                        sqf, xslf, AF.Square),
                        wait=[ltok, tmark.get(("a1s", scn * 8 + dt - 2))])
                    sqv = (sc, sc.v)
                    for sub in range(2):
                        E("t", lambda e, par=par, sub=sub, xsl=xsl, dt=dt:
                            e.matmul(bank(par * 2 + sub), onesfr_sb,
                                     xsl[:, sub * 512:sub * 512 + 512],
                                     start=(dt == 0), stop=(dt == 7),
                                     skip_group_check=True),
                            wait=([ltok, k_consts] +
                                  ([vmark.get(("a1ev", scn - 2))]
                                   if dt == 0 else [])))
                    for sub in range(2):
                        E("t", lambda e, par=par, sub=sub, sq=sq, dt=dt:
                            e.matmul(bank(4 + par * 2 + sub), onesfr_sb,
                                     sq[:, sub * 512:sub * 512 + 512],
                                     start=(dt == 0), stop=(dt == 7),
                                     skip_group_check=True),
                            wait=[sqv] +
                                 ([vmark.get(("a1ev", scn - 2))]
                                  if dt == 0 else []),
                            inc=[(tc, 1)])
                    tmark[("a1s", scn * 8 + dt)] = (tc, tc.v)
                vmark[("a1sq", gb)] = (sc, sc.v)
                tmark[("a1", gb)] = (tc, tc.v)
            done = (tc, tc.v)
            vs = a_vs[scn % 2]
            mbf = a_mbf[scn % 2]
            E("v", lambda e, par=par, mbf=mbf: e.tensor_scalar(
                mbf, ps[:, par * 1024:par * 1024 + 1024],
                1.0 / 1024, None, ALU.mult),
                wait=[done, smark.get(("a1i", scn - 2))])
            E("v", lambda e, vs=vs, mbf=mbf: e.tensor_tensor(
                vs, mbf, mbf, ALU.mult))
            E("v", lambda e, par=par, vs=vs: e.scalar_tensor_tensor(
                vs, ps[:, 2048 + par * 1024:2048 + par * 1024 + 1024],
                1.0 / 1024, vs, ALU.mult, ALU.subtract))
            vmark[("a1ev", scn)] = (vc, vc.v)
            E("s", lambda e, c0=c0, mbf=mbf: e.copy(
                mbb_sb[:, c0:c0 + 1024], mbf), wait=[(vc, vc.v)])
            E("s", lambda e, vs=vs: e.activation(
                vs, vs, AF.Ln, bias=eps_sb))
            E("s", lambda e, c0=c0, vs=vs: e.activation(
                isbb_sb[:, c0:c0 + 1024], vs, AF.Exp, scale=-0.5))
            smark[("a1i", scn)] = (sc, sc.v)
            if scn % 2 == 1:
                emit_a2_chunk(scn // 2)

        em.phase = "a2"

        em.phase = "a3"
        last_kb = {}
        for tcn in range(16):
            c0 = tcn * 512
            lc = a3_ln1c[tcn % 2]
            ltok = D("g", qK, lambda e, c0=c0, lc=lc: e.dma_start(
                out=lc.rearrange("p (dt c) -> p dt c", dt=8),
                in_=ln1_s[:, :, c0:c0 + 512].rearrange("dt p c -> p dt c")),
                wait=[omark[("a2", (tcn // 4) * 8 + dd)] for dd in range(8)]
                + ([tmark.get(("a3", tcn - 2))] if tcn >= 2 else []))
            nft = 16 if tcn < 8 else 8
            for fi in range(nft):
                ftr = fi if tcn < 8 else 8 + fi
                bk = fi % 2
                for dt in range(8):
                    E("t", lambda e, bk=bk, dt=dt, ftr=ftr, lc=lc:
                        e.matmul(bank(bk),
                                 WT_sb[:, dt * 3072 + ftr * 128:
                                       dt * 3072 + ftr * 128 + 128],
                                 lc[:, dt * 512:dt * 512 + 512],
                                 start=(dt == 0), stop=(dt == 7),
                                 skip_group_check=True),
                        wait=([ltok, wt_tok,
                               (sc, smark[("a3qk", tcn, fi - 2)][1]
                                if fi >= 2 else sc.v)] +
                              ([vmark[("a1ev", 6)], vmark[("a1ev", 7)]]
                               if (tcn == 0 and fi < 2) else [])
                              if dt == 0 else []),
                        inc=([(tc, 1)] if dt == 7 else []))
                if ftr < 8:
                    E("s", lambda e, bk=bk, ftr=ftr, c0=c0: e.add(
                        qT[:, ftr * 4096 + c0: ftr * 4096 + c0 + 512],
                        bank(bk), bqk_sb[:, ftr:ftr + 1]),
                        wait=[(tc, tc.v)])
                else:
                    kb = a3_kbuf[fi % 2]
                    E("s", lambda e, bk=bk, ftr=ftr, kb=kb: e.add(
                        kb, bank(bk), bqk_sb[:, ftr:ftr + 1]),
                        wait=[(tc, tc.v), last_kb.get(fi % 2)])
                    last_kb[fi % 2] = D("y", qO,
                        lambda e, ftr=ftr, c0=c0, kb=kb: e.dma_start(
                            out=kT_s[ftr - 8, :, c0:c0 + 512], in_=kb),
                        wait=[(sc, sc.v)])
                smark[("a3qk", tcn, fi)] = (sc, sc.v)
            for tt4 in range(4):
                ttg = tcn * 4 + tt4
                vb = (2, 3) if tt4 % 2 == 0 else (4, 5)
                for kc in range(2):
                    for dt in range(8):
                        E("t", lambda e, vb=vb, kc=kc, dt=dt, tt4=tt4, lc=lc:
                            e.matmul(bank(vb[kc]),
                                     lc[:, dt * 512 + tt4 * 128:
                                        dt * 512 + tt4 * 128 + 128],
                                     WT_sb[:, dt * 3072 + 2048 + kc * 512:
                                           dt * 3072 + 2048 + kc * 512 + 512],
                                     start=(dt == 0), stop=(dt == 7),
                                     skip_group_check=True),
                            wait=([vmark.get(("a3v", ttg - 2))] +
                                  ([vmark[("a1ev", 6)], vmark[("a1ev", 7)]]
                                   if ttg < 2 else [])
                                  if (dt == 0 and kc == 0) else []),
                            inc=([(tc, 1)] if (dt == 7 and kc == 1) else []))
                vbuf = a3_vbuf[tt4 % 2]
                for kc in range(2):
                    E("v", lambda e, vb=vb, kc=kc, vbuf=vbuf: e.tensor_tensor(
                        vbuf[:, kc * 512:kc * 512 + 512], bank(vb[kc]),
                        bvb_sb[:, kc * 512:kc * 512 + 512], ALU.add),
                        wait=[(tc, tc.v), omark.get(("a3vs", ttg - 2))])
                vmark[("a3v", ttg)] = (vc, vc.v)
                omark[("a3vs", ttg)] = D("y", qO,
                    lambda e, ttg=ttg, vbuf=vbuf: e.dma_start(
                        out=v_s[ttg], in_=vbuf),
                    wait=[(vc, vc.v)])
            tmark[("a3", tcn)] = (tc, tc.v)
        a3_t = (tc, tc.v)
        a3_s = (sc, sc.v)
        a3_v = (vc, vc.v)
        a3_st = qO.barrier()

        em.phase = "b"
        negm_all = f32(T4, 2176, 64)
        zB_all = f32(T4, 2240, 64)
        ztmp = mtmp[:, 3:4]
        expm = {}
        bmark = {}
        for h in range(2):
            for jt in range(64):
                ring = kt_ring[(jt // 2) % 2]
                if jt % 2 == 0:
                    ktok = D("g", qK, lambda e, jt=jt, ring=ring: e.dma_start(
                        out=ring.rearrange("p (dt c) -> p dt c", dt=8),
                        in_=kT_s[:, :, jt * 128:jt * 128 + 256]
                        .rearrange("dt p c -> p dt c")),
                        wait=([a3_st] if (h == 0 and jt < 4)
                              else [bmark.get((h, jt - 3),
                                    bmark.get((h - 1, 61 + jt)))]))
                ebuf = eb[jt % 2]
                slot = (jt % 2) * 2048
                for c2 in range(2):
                    for dt in range(8):
                        for sub in range(2):
                            w = []
                            if dt == 0 and sub == 0 and c2 == 0:
                                w = [ktok, expm.get((h, jt - 2),
                                                    expm.get((h - 1, 62 + jt)))]
                                if h == 0 and jt < 2:
                                    w += [a3_s, a3_v]
                            E("t", lambda e, c2=c2, dt=dt, sub=sub, ring=ring,
                                slot=slot, h=h, jt=jt: e.matmul(
                                    ps[:, slot + c2 * 1024 + sub * 512:
                                       slot + c2 * 1024 + sub * 512 + 512],
                                    ring[:, dt * 256 + (jt % 2) * 128:
                                         dt * 256 + (jt % 2) * 128 + 128],
                                    qT[:, dt * 4096 + h * 2048 + c2 * 1024
                                       + sub * 512:
                                       dt * 4096 + h * 2048 + c2 * 1024
                                       + sub * 512 + 512],
                                    start=(dt == 0), stop=(dt == 7),
                                    skip_group_check=True),
                                wait=w,
                                inc=([(tc, 1)] if (dt == 7 and sub == 1)
                                     else []))
                mmdone = (tc, tc.v)
                bmark[(h, jt)] = mmdone
                if h == 0:
                    E("v", lambda e, slot=slot: e.tensor_reduce(
                        mtmp[:, 0:1], ps[:, slot:slot + 1024], AX, ALU.max),
                        wait=[mmdone, expm.get((0, jt - 2))])
                    E("v", lambda e, jt=jt: e.tensor_scalar(
                        negm_all[:, jt:jt + 1], mtmp[:, 0:1], -1.0 / 1024,
                        0.0, ALU.mult, ALU.min))
                    negm_v = (vc, vc.v)
                    zout = z_all[:, jt:jt + 1]
                else:
                    negm_v = None
                    zout = zB_all[:, jt:jt + 1]
                E("s", lambda e, ebuf=ebuf, slot=slot, jt=jt, zout=zout:
                    e.activation(
                        ebuf, ps[:, slot:slot + 2048],
                        AF.Exp, scale=1.0 / 1024,
                        bias=negm_all[:, jt:jt + 1], accum_out=zout),
                    wait=[mmdone, negm_v,
                          omark.get(("bE", h, jt - 2),
                                    omark.get(("bE", h - 1, 62 + jt)))] +
                         ([vmark.get(("bz", 0, jt - 1),
                                     vmark.get(("bz", -1, 63 + jt)))]
                          if h == 0 else []))
                expm[(h, jt)] = (sc, sc.v)
                if h == 0:
                    E("v", lambda e, jt=jt: e.tensor_scalar(
                        m_all[:, jt:jt + 1], negm_all[:, jt:jt + 1], -1.0,
                        None, ALU.mult), wait=[negm_v])
                    vmark[("bz", h, jt)] = (vc, vc.v)
                else:
                    pass
                omark[("bE", h, jt)] = D("y", qO,
                    lambda e, jt=jt, ebuf=ebuf, h=h: e.dma_start(
                        out=E_s[h * 4:h * 4 + 4, jt]
                        .rearrange("ic p c -> p ic c"),
                        in_=ebuf.rearrange("p (ic c) -> p ic c", ic=4)),
                    wait=[expm[(h, jt)]])
        E("v", lambda e: e.tensor_tensor(z_all, z_all, zB_all, ALU.add),
            wait=[(sc, sc.v)])
        b_t = (tc, tc.v)
        b_s = (sc, sc.v)
        b_v = (vc, vc.v)

        em.phase = "x"
        t1 = D("y", qO, lambda e: e.dma_start(out=cc_in[:, 0:64], in_=m_all),
               wait=[b_v])
        t2 = D("y", qO, lambda e: e.dma_start(out=cc_in[:, 64:128],
                                              in_=z_all), wait=[b_v])
        E("g", lambda e: e.collective_compute(
            "AllGather", ALU.bypass, ins=[cc_in], outs=[cc_out],
            replica_groups=replica_groups),
            wait=[t1, t2], inc=[(cc, 1)], dma=True)
        g1 = D("g", qH, lambda e: e.dma_start(out=s0_sb,
                                              in_=cc_out[0:128, :]),
               wait=[(cc, cc.v)])
        g2 = D("g", qH, lambda e: e.dma_start(out=s1_sb,
                                              in_=cc_out[128:256, :]),
               wait=[(cc, cc.v)])
        E("v", lambda e: e.tensor_tensor(
            cm[:, 0:128], s0_sb, flag_sb, ALU.mult), wait=[g1, g2])
        E("v", lambda e: e.tensor_scalar(
            cm[:, 128:256], flag_sb, -1.0, 1.0, ALU.mult, ALU.add))
        E("v", lambda e: e.tensor_tensor(
            cm[:, 128:256], cm[:, 128:256], s1_sb, ALU.mult))
        E("v", lambda e: e.tensor_tensor(
            cm[:, 0:128], cm[:, 0:128], cm[:, 128:256], ALU.add))
        for half in range(2):
            my = m_all[:, half * 32:half * 32 + 32]
            pa = cm[:, (1 - half) * 32:(1 - half) * 32 + 32]
            E("v", lambda e, my=my, pa=pa, half=half: e.tensor_tensor(
                cm[:, 128 + half * 32:128 + half * 32 + 32], my, pa,
                ALU.max))
        mg = cm[:, 128:192]
        E("v", lambda e: e.tensor_tensor(
            cm[:, 192:256], m_all, mg, ALU.subtract))
        for half in range(2):
            pa = cm[:, (1 - half) * 32:(1 - half) * 32 + 32]
            E("v", lambda e, pa=pa, half=half: e.tensor_tensor(
                cm[:, 256 + half * 32:256 + half * 32 + 32], pa,
                mg[:, half * 32:half * 32 + 32], ALU.subtract))
        E("s", lambda e: e.activation(cm[:, 192:256], cm[:, 192:256],
                                      AF.Exp), wait=[(vc, vc.v)])
        E("s", lambda e: e.activation(cm[:, 256:320], cm[:, 256:320],
                                      AF.Exp))
        e01 = (sc, sc.v)
        E("v", lambda e: e.tensor_tensor(
            mg, z_all, cm[:, 192:256], ALU.mult), wait=[e01])
        for half in range(2):
            pz_ = cm[:, 64 + (1 - half) * 32:64 + (1 - half) * 32 + 32]
            E("v", lambda e, pz_=pz_, half=half: e.tensor_tensor(
                cm[:, 256 + half * 32:256 + half * 32 + 32], pz_,
                cm[:, 256 + half * 32:256 + half * 32 + 32], ALU.mult))
        E("v", lambda e: e.tensor_tensor(
            mg, mg, cm[:, 256:320], ALU.add))
        E("v", lambda e: e.reciprocal(cm[:, 256:320], mg))
        E("v", lambda e: e.tensor_tensor(
            fj, cm[:, 192:256], cm[:, 256:320], ALU.mult))
        fj_v = (vc, vc.v)

        em.phase = "c"
        vp_v = (vc, vc.v)
        evD = {k: None for k in range(8)}
        for ic in range(8):
            cx_in = cx_inb[ic % 2]
            cxtok = D("y", qA, lambda e, ic=ic, cx_in=cx_in: e.dma_start(
                out=cx_in.rearrange("p (kt c) -> p kt c", kt=8),
                in_=xT[:, :, ic * 512:ic * 512 + 512]
                .rearrange("kt p c -> p kt c")),
                wait=[vmark.get(("cev", ic - 1))])
            for jt in range(64):
                if ic == 0:
                    arena = vpA if jt < 32 else vpB
                    vt = arena[:, (jt % 32) * 1024:(jt % 32) * 1024 + 1024]
                    vtok = D("y", qA, lambda e, jt=jt, vt=vt: e.dma_start(
                        out=vt, in_=v_s[jt]),
                        wait=[fj_v, b_t, omark[("a3vs", jt)],
                              vmark.get(("vp", jt - 8))] +
                             ([omark[("bE", 1, 62)], omark[("bE", 1, 63)]]
                              if jt >= 32 else []))
                    E("s", lambda e, jt=jt, vt=vt: e.mul(
                        vt, vt, fj[:, jt:jt + 1]),
                        wait=[vtok, fj_v])
                    vmark[("vp", jt)] = (sc, sc.v)
                er = e_ring[jt % 8]
                etok = D("s", qB, lambda e, ic=ic, jt=jt, er=er: e.dma_start(
                    out=er, in_=E_s[ic, jt]),
                    wait=[omark[("bE", ic // 4, jt)] if ic in (0, 4)
                          else None,
                          tmark.get(("c", ic, jt - 8),
                                    tmark.get(("c", ic - 1, 56 + jt)))])
                arena = vpA if jt < 32 else vpB
                col = (jt % 32) * 1024
                for kt in range(8):
                    w = []
                    if kt == 0:
                        w = [etok,
                             vmark[("vp", jt)] if ic == 0 else None]
                    if jt == 0:
                        w = w + [evD[kt]] + ([b_s] if ic == 0 else [])
                    E("t", lambda e, kt=kt, er=er, arena=arena, col=col,
                        jt=jt: e.matmul(
                            bank(kt),
                            arena[:, col + kt * 128: col + kt * 128 + 128],
                            er, start=(jt == 0), stop=(jt == 63),
                            skip_group_check=True),
                        wait=w,
                        inc=[(tc, 1)] if (kt == 7 or jt == 63) else [])
                tmark[("c", ic, jt)] = (tc, tc.v)
            tl = (tc, tc.v)
            E("v", lambda e, cx_in=cx_in: e.tensor_tensor(
                cx2[:, 0:2048], ps[:, 0:2048], cx_in[:, 0:2048], ALU.add),
                wait=[(tc, tl[1] - 4), cxtok, omark.get(("cx2", ic - 1))])
            for kt in range(4):
                evD[kt] = (vc, vc.v)
            E("v", lambda e, cx_in=cx_in: e.tensor_tensor(
                cx2[:, 2048:4096], ps[:, 2048:4096], cx_in[:, 2048:4096],
                ALU.add), wait=[tl])
            for kt in range(4, 8):
                evD[kt] = (vc, vc.v)
            vmark[("cev", ic)] = (vc, vc.v)
            omark[("cx2", ic)] = D("y", qO, lambda e, ic=ic: e.dma_start(
                out=x2_s[:, :, ic * 512:ic * 512 + 512]
                .rearrange("dt p c -> p dt c"),
                in_=cx2.rearrange("p (kt c) -> p kt c", kt=8)),
                wait=[(vc, vc.v)])
        c_st = qO.barrier()

        em.phase = "d"
        sel_tok = D("y", qC, lambda e: e.dma_start(out=SEL, in_=sel_in),
                    wait=[omark[("a2", 28 + dd)] for dd in range(4)])
        for scn in range(4):
            par = scn % 2
            c0 = scn * 1024
            for dt in range(8):
                g = scn * 8 + dt
                xb = xbf2[g % 2]
                sq = sq2[g % 2]
                ldt = D("g", qW, lambda e, dt=dt, c0=c0, xb=xb: e.dma_start(
                    out=xb, in_=x2_s[dt, :, c0:c0 + 1024]),
                    wait=[omark[("cx2", scn * 2)],
                          omark[("cx2", scn * 2 + 1)],
                          tmark.get(("d", g - 2)),
                          vmark.get(("dsq", g - 2))])
                E("v", lambda e, xb=xb, sq=sq: e.tensor_tensor(
                    sq, xb, xb, ALU.mult), wait=[ldt])
                vmark[("dsq", g)] = (vc, vc.v)
                for sub in range(2):
                    E("t", lambda e, par=par, sub=sub, xb=xb, dt=dt:
                        e.matmul(bank(par * 2 + sub), ones_sb,
                                 xb[:, sub * 512:sub * 512 + 512],
                                 start=(dt == 0), stop=(dt == 7),
                                 skip_group_check=True),
                        wait=[ldt] +
                             ([evD[par * 2 + sub],
                               vmark.get(("dev", scn - 2))]
                              if dt == 0 else []))
                for sub in range(2):
                    E("t", lambda e, par=par, sub=sub, sq=sq, dt=dt:
                        e.matmul(bank(4 + par * 2 + sub), ones_sb,
                                 sq[:, sub * 512:sub * 512 + 512],
                                 start=(dt == 0), stop=(dt == 7),
                                 skip_group_check=True),
                        wait=[vmark[("dsq", g)]] +
                             ([evD[4 + par * 2 + sub],
                               vmark.get(("dev", scn - 2))]
                              if dt == 0 else []),
                        inc=[(tc, 1)])
                tmark[("d", g)] = (tc, tc.v)
            done = (tc, tc.v)
            E("v", lambda e, par=par, c0=c0: e.tensor_scalar(
                mb2_sb[:, c0:c0 + 1024], ps[:, par * 1024:par * 1024 + 1024],
                1.0 / 1024, None, ALU.mult), wait=[done, vp_v])
            E("v", lambda e, c0=c0: e.tensor_tensor(
                isb2_sb[:, c0:c0 + 1024], mb2_sb[:, c0:c0 + 1024],
                mb2_sb[:, c0:c0 + 1024], ALU.mult))
            E("v", lambda e, par=par, c0=c0: e.scalar_tensor_tensor(
                isb2_sb[:, c0:c0 + 1024],
                ps[:, 2048 + par * 1024:2048 + par * 1024 + 1024],
                1.0 / 1024, isb2_sb[:, c0:c0 + 1024], ALU.mult,
                ALU.subtract))
            vmark[("dev", scn)] = (vc, vc.v)
            E("s", lambda e, c0=c0: e.activation(
                isb2_sb[:, c0:c0 + 1024], isb2_sb[:, c0:c0 + 1024], AF.Ln,
                bias=eps_sb), wait=[(vc, vc.v)])
            E("s", lambda e, c0=c0: e.activation(
                isb2_sb[:, c0:c0 + 1024], isb2_sb[:, c0:c0 + 1024], AF.Exp,
                scale=-0.5))
        ln2_s_m = (sc, sc.v)

        for q4 in range(4):
            E("v", lambda e, q4=q4: e.tensor_scalar(
                mbb2_sb[:, q4 * 1024:q4 * 1024 + 1024],
                mb2_sb[:, q4 * 1024:q4 * 1024 + 1024], 1.0, None, ALU.mult),
                wait=[(sc, ln2_s_m[1])] if q4 == 0 else [])
            E("v", lambda e, q4=q4: e.tensor_scalar(
                isbb2_sb[:, q4 * 1024:q4 * 1024 + 1024],
                isb2_sb[:, q4 * 1024:q4 * 1024 + 1024], 1.0, None, ALU.mult))
        cvt_v = (vc, vc.v)
        xb2l = [b16(T3, i * 2048, 2048) for i in range(4)]
        for tn in range(2):
            for dt in range(8):
                g = tn * 8 + dt
                c0 = tn * 2048
                xbt = xb2l[g % 4]
                xtok = D("g", qW, lambda e, dt=dt, c0=c0, xbt=xbt:
                    e.dma_start(out=xbt, in_=x2_s[dt, :, c0:c0 + 2048]),
                    wait=[vmark.get(("dap", g - 4)), (tc, tc.v),
                          vmark[("cev", 7)]])
                E("v", lambda e, xbt=xbt, c0=c0: e.tensor_tensor(
                    xbt, xbt, mbb2_sb[:, c0:c0 + 2048], ALU.subtract),
                    wait=[xtok, cvt_v])
                E("v", lambda e, xbt=xbt, dt=dt, c0=c0: e.tensor_tensor(
                    xf[:, dt * 4096 + c0:dt * 4096 + c0 + 2048], xbt,
                    isbb2_sb[:, c0:c0 + 2048], ALU.mult))
                vmark[("dap", g)] = (vc, vc.v)
            vmark[("dapH", tn)] = (vc, vc.v)
        xf_v = (vc, vc.v)

        for tt in range(32):
            bk = tt % 8
            for dt in range(8):
                E("t", lambda e, tt=tt, dt=dt, bk=bk: e.matmul(
                    ps[:, bk * 512:bk * 512 + 8],
                    xf[:, dt * 4096 + tt * 128:dt * 4096 + tt * 128 + 128],
                    rT_sb[:, dt * 8:dt * 8 + 8],
                    start=(dt == 0), stop=(dt == 7), skip_group_check=True),
                    wait=([vmark[("dapH", tt // 16)],
                           vmark.get(("rt", tt - 8))]
                          if dt == 0 else []),
                    inc=([(tc, 1)] if dt == 7 else []))
            E("v", lambda e, tt=tt, bk=bk: e.tensor_copy(
                lg_sb[:, tt * 8:tt * 8 + 8], ps[:, bk * 512:bk * 512 + 8]),
                wait=[(tc, tc.v)])
            vmark[("rt", tt)] = (vc, vc.v)
        lg_v = (vc, vc.v)

        lg3 = lg_sb.rearrange("p (t e) -> p t e", e=8)
        ge1 = rtmp[:, 0:256]
        pzt = rtmp[:, 256:512]
        ge2 = rtmp[:, 512:768]
        m2c = rtmp[:, 768:800]
        denc = rtmp[:, 800:832]
        exl = rtmp[:, 832:1088]
        mxc = cm[:, 0:32]
        E("v", lambda e: e.tensor_reduce(mxc.unsqueeze(2), lg3, AX, ALU.max),
            wait=[lg_v])
        E("v", lambda e: e.tensor_tensor(
            ge1.rearrange("p (t e) -> p t e", e=8), lg3,
            mxc.unsqueeze(2).broadcast_to([128, 32, 8]), ALU.is_ge))
        E("v", lambda e: e.tensor_tensor(
            exl.rearrange("p (t e) -> p t e", e=8), lg3,
            mxc.unsqueeze(2).broadcast_to([128, 32, 8]), ALU.subtract))
        E("s", lambda e: e.activation(exl, exl, AF.Exp), wait=[(vc, vc.v)])
        ex_s = (sc, sc.v)
        E("v", lambda e: e.scalar_tensor_tensor(
            pzt, ge1, -1.0e30, lg_sb, ALU.mult, ALU.add))
        E("v", lambda e: e.tensor_reduce(
            m2c.unsqueeze(2), pzt.rearrange("p (t e) -> p t e", e=8), AX,
            ALU.max))
        E("v", lambda e: e.tensor_tensor(
            ge2.rearrange("p (t e) -> p t e", e=8), lg3,
            m2c.unsqueeze(2).broadcast_to([128, 32, 8]), ALU.is_ge))
        E("v", lambda e: e.tensor_tensor(cw_sb, exl, ge2, ALU.mult),
            wait=[ex_s])
        E("v", lambda e: e.tensor_reduce(
            denc.unsqueeze(2), cw_sb.rearrange("p (t e) -> p t e", e=8), AX,
            ALU.add))
        E("v", lambda e: e.reciprocal(m2c, denc))
        E("v", lambda e: e.tensor_tensor(
            cw_sb.rearrange("p (t e) -> p t e", e=8),
            cw_sb.rearrange("p (t e) -> p t e", e=8),
            m2c.unsqueeze(2).broadcast_to([128, 32, 8]), ALU.mult))
        cw_v = (vc, vc.v)

        for tt in range(32):
            bkq = (tt % 4) * 512
            E("t", lambda e, tt=tt, bkq=bkq: e.transpose(
                ps[0:8, bkq:bkq + 128], cw_sb[:, tt * 8:tt * 8 + 8],
                ident_sb),
                wait=[cw_v if tt < 4 else vmark[("ct", tt - 4)]],
                inc=[(tc, 1)])
            E("v", lambda e, tt=tt, bkq=bkq: e.tensor_copy(
                CWT[:, tt * 128:tt * 128 + 128], ps[0:8, bkq:bkq + 128]),
                wait=[(tc, tc.v)])
            vmark[("ct", tt)] = (vc, vc.v)
        cwt_v = (vc, vc.v)

        em.phase = "d1"
        def wg_ap(e_):
            return wg[e_] if e_ < 8 else wgs

        def wu_ap(e_):
            return wu[e_] if e_ < 8 else wus

        def wd_ap(e_):
            return wd[e_] if e_ < 8 else wds

        wld = {}
        t1 = D("g", qW, lambda e: e.dma_start(
            out=wgu_cur[0].rearrange("p (ht i) -> p ht i", ht=8),
            in_=wg[0].rearrange("ht p i -> p ht i")),
            wait=[xf_v, c_st])
        t2 = D("g", qW, lambda e: e.dma_start(
            out=wgu_cur[1].rearrange("p (ht i) -> p ht i", ht=8),
            in_=wu[0].rearrange("ht p i -> p ht i")))
        wld[0] = [t1, t2]
        d1s = (sc, sc.v)
        d1v = (vc, vc.v)
        hpm = {}
        for en in range(9):
            cur = wgu_cur if en % 2 == 0 else wgu_nxt
            nxt = wgu_nxt if en % 2 == 0 else wgu_cur
            if en < 8:
                t1 = D("g", qW, lambda e, en=en, nxt=nxt: e.dma_start(
                    out=nxt[0].rearrange("p (ht i) -> p ht i", ht=8),
                    in_=wg_ap(en + 1).rearrange("ht p i -> p ht i")),
                    wait=[(tc, tc.v)] + ([xf_v] if en == 0 else []))
                t2 = D("g", qW, lambda e, en=en, nxt=nxt: e.dma_start(
                    out=nxt[1].rearrange("p (ht i) -> p ht i", ht=8),
                    in_=wu_ap(en + 1).rearrange("ht p i -> p ht i")))
                wld[en + 1] = [t1, t2]
            hpm[("wd", en)] = D("g", qH, lambda e, en=en: e.dma_start(
                out=wdb_s[en], in_=wd_ap(en)))
            if en < 8:
                for ch in range(8):
                    E("t", lambda e, en=en, ch=ch: e.matmul(
                        bank(ch % 4), SEL[:, en * 128:en * 128 + 128],
                        CWT[:, ch * 512:ch * 512 + 512],
                        start=True, stop=True, skip_group_check=True),
                        wait=[cwt_v, d1s, sel_tok,
                              vmark.get(("cwb", en, ch - 4))],
                        inc=[(tc, 1)])
                    E("v", lambda e, ch=ch: e.tensor_copy(
                        cwb_sb[:, ch * 512:ch * 512 + 512], bank(ch % 4)),
                        wait=[(tc, tc.v)])
                    vmark[("cwb", en, ch)] = (vc, vc.v)
            cwb_v = (vc, vc.v)
            for it in range(NIT):
                for th in range(2):
                    g = (en * NIT + it) * 2 + th
                    par = g % 2
                    gs_t = d1_gs[par]
                    up_t = d1_up[par]
                    for ht in range(8):
                        for sub in range(4):
                            w = []
                            if ht == 0 and sub == 0:
                                w = wld[en] + [cwb_v, d1s, xf_v]
                            E("t", lambda e, cur=cur, ht=ht, sub=sub, it=it,
                                th=th: e.matmul(
                                    bank(sub),
                                    cur[0][:, ht * NI + it * 128:
                                           ht * NI + it * 128 + 128],
                                    xf[:, ht * 4096 + th * 2048 + sub * 512:
                                       ht * 4096 + th * 2048 + sub * 512 + 512],
                                    start=(ht == 0), stop=(ht == 7),
                                    skip_group_check=True),
                                wait=w,
                                inc=([(tc, 1)] if (ht == 7 and sub == 3)
                                     else []))
                    g_t = (tc, tc.v)
                    for ht in range(8):
                        for sub in range(4):
                            w = []
                            if ht == 0 and sub == 0:
                                w = [d1v]
                            E("t", lambda e, cur=cur, ht=ht, sub=sub, it=it,
                                th=th: e.matmul(
                                    bank(4 + sub),
                                    cur[1][:, ht * NI + it * 128:
                                           ht * NI + it * 128 + 128],
                                    xf[:, ht * 4096 + th * 2048 + sub * 512:
                                       ht * 4096 + th * 2048 + sub * 512 + 512],
                                    start=(ht == 0), stop=(ht == 7),
                                    skip_group_check=True),
                                wait=w,
                                inc=([(tc, 1)] if (ht == 7 and sub == 3)
                                     else []))
                    u_t = (tc, tc.v)
                    E("s", lambda e, gs_t=gs_t: e.activation(
                        gs_t, ps[:, 0:2048], AF.Silu),
                        wait=[g_t, vmark.get(("d1h", g - 2))])
                    d1s = (sc, sc.v)
                    if en < 8:
                        E("v", lambda e, up_t=up_t, th=th: e.tensor_tensor(
                            up_t, ps[:, 2048:4096],
                            cwb_sb[:, th * 2048:th * 2048 + 2048], ALU.mult),
                            wait=[u_t, hpm.get(("hp", g - 2))])
                    else:
                        E("v", lambda e, up_t=up_t: e.tensor_scalar(
                            up_t, ps[:, 2048:4096], 1.0, None, ALU.mult),
                            wait=[u_t, hpm.get(("hp", g - 2))])
                    d1v = (vc, vc.v)
                    E("v", lambda e, gs_t=gs_t, up_t=up_t: e.tensor_tensor(
                        up_t, gs_t, up_t, ALU.mult),
                        wait=[d1s])
                    vmark[("d1h", g)] = (vc, vc.v)
                    hpm[("hp", g)] = D("y", qO,
                        lambda e, en=en, it=it, th=th, up_t=up_t:
                        e.dma_start(
                            out=hp_s[en, it, :, th * 2048:th * 2048 + 2048],
                            in_=up_t),
                        wait=[(vc, vc.v)])
        d1_t = (tc, tc.v)
        d1_fs = (sc, sc.v)
        d1_fv = (vc, vc.v)

        em.phase = "d2"
        d2_x2b = f32(T1, 0, 4096)
        d2_x3b = f32(T1, 4096, 4096)
        hrF = [b16(T2, 0, 5632), b16(T2, 5632, 5632)]
        wdF = [b16(T3, 0, 11264), b16(T3, 11264, 11264)]
        wdR = [b16(T1, 16384 + i * 11264, 11264) for i in range(1)] + \
              [b16(T2, 11264 + i * 11264, 11264) for i in range(1)] + \
              [b16(T4, 6528, 11264 - 2048)] + []
        wdR = wdR[:2]
        wdr_tok = []
        for i in range(2):
            wdr_tok.append(D("s", qB, lambda e, i=i, w=wdR[i]: e.dma_start(
                out=w.rearrange("p (it c) -> p it c", it=NIT),
                in_=wdb_s[i].rearrange("it p c -> p it c")),
                wait=[hpm[("wd", i)], d1_t]))
        yev = {k: None for k in range(8)}
        d2m = {}
        for tn in range(8):
            c0 = tn * 512
            x2tok = D("y", qA, lambda e, c0=c0: e.dma_start(
                out=d2_x2b.rearrange("p (kt c) -> p kt c", kt=8),
                in_=x2_s[:, :, c0:c0 + 512].rearrange("dt p c -> p dt c")),
                wait=[vmark.get(("d2ev", tn - 1)), d1_t if tn == 0 else None])
            for en in range(9):
                gg = tn * 9 + en
                par = gg % 2
                hr = hrF[par]
                wr = wdF[par]
                hw = [hpm[("hp", (en * NIT + i2) * 2 + t2)]
                      for i2 in range(NIT) for t2 in range(2)]
                htok = D("s", qB, lambda e, en=en, c0=c0, hr=hr: e.dma_start(
                    out=hr.rearrange("p (it c) -> p it c", it=NIT),
                    in_=hp_s[en, :, :, c0:c0 + 512]
                    .rearrange("it p c -> p it c")),
                    wait=hw + [d2m.get(gg - 2),
                               d1_t if gg < 2 else None])
                if en < 2:
                    wr = wdR[en]
                    wtok = wdr_tok[en]
                else:
                    wtok = D("s", qB, lambda e, en=en, wr=wr: e.dma_start(
                        out=wr.rearrange("p (it c) -> p it c", it=NIT),
                        in_=wdb_s[en].rearrange("it p c -> p it c")),
                        wait=[hpm[("wd", en)], d2m.get(gg - 2),
                              d1_t if gg < 2 else None])
                first = (en == 0)
                last = (en == 8)
                for it in range(NIT):
                    for ht in range(8):
                        w = []
                        if it == 0 and ht == 0:
                            w = [htok, wtok]
                        if first and it == 0:
                            w = w + [yev[ht]] + \
                                ([d1_fs, d1_fv] if tn == 0 else [])
                        E("t", lambda e, it=it, ht=ht, hr=hr, wr=wr,
                            first=first, last=last: e.matmul(
                                bank(ht),
                                wr[:, it * 1024 + ht * 128:
                                   it * 1024 + ht * 128 + 128],
                                hr[:, it * 512:it * 512 + 512],
                                start=(first and it == 0),
                                stop=(last and it == NIT - 1),
                                skip_group_check=True),
                            wait=w,
                            inc=([(tc, 1)] if (ht == 7 and it == NIT - 1)
                                 else []))
                d2m[gg] = (tc, tc.v)
            tl = (tc, tc.v)
            for ht in range(8):
                E("v", lambda e, ht=ht: e.tensor_tensor(
                    d2_x3b[:, ht * 512:ht * 512 + 512], bank(ht),
                    d2_x2b[:, ht * 512:ht * 512 + 512], ALU.add),
                    wait=[tl, x2tok, omark.get(("out", tn - 1))])
                yev[ht] = (vc, vc.v)
            vmark[("d2ev", tn)] = (vc, vc.v)
            omark[("out", tn)] = D("y", qO, lambda e, c0=c0: e.dma_start(
                out=out[:, :, c0:c0 + 512].rearrange("dt p c -> p dt c"),
                in_=d2_x3b.rearrange("p (kt c) -> p kt c", kt=8)),
                wait=[(vc, vc.v)])
        fin = qO.barrier()

        E("g", lambda e: [e.wait_ge(c.h, v) for (c, v) in fin][-1], dma=True)

        def replay(e, key):
            items = em.q[key]
            i = 0
            while i < len(items):
                ph = items[i][0]
                j = i
                with nc.named_scope(f"{ph}_{key}"):
                    while j < len(items) and items[j][0] == ph:
                        items[j][1](e)
                        j += 1
                i = j

        with nc.Block() as block:
            @block.sync
            def _(e):
                replay(e, "y")

            @block.tensor
            def _(e):
                replay(e, "t")

            @block.vector
            def _(e):
                replay(e, "v")

            @block.scalar
            def _(e):
                replay(e, "s")

            @block.gpsimd
            def _(e):
                replay(e, "g")

    return nc


def _prep_inputs(L0_attn_x, L0_attn_W, L0_attn_b, L0_ffn_w_router, L0_ffn_wg,
                 L0_ffn_wu, L0_ffn_wd, L0_ffn_wg_s, L0_ffn_wu_s, L0_ffn_wd_s):
    bf = ml_dtypes.bfloat16
    x = np.asarray(L0_attn_x, np.float32)
    W = np.asarray(L0_attn_W, np.float32)
    b = np.asarray(L0_attn_b, np.float32)
    WTm = np.ascontiguousarray(W.T).reshape(8, 128, 3072)
    bqk = np.ascontiguousarray(b[0:2048].reshape(16, 128).T)
    bvb = np.ascontiguousarray(np.tile(b[2048:3072], (128, 1))).astype(bf)
    rTm = np.ascontiguousarray(
        np.asarray(L0_ffn_w_router, np.float32).T.reshape(8, 128, 8)
        .transpose(1, 0, 2).reshape(128, 64)).astype(bf)
    wg_m = np.ascontiguousarray(np.asarray(L0_ffn_wg, np.float32)
                                .reshape(NE, 8, 128, NI))
    wu_m = np.ascontiguousarray(np.asarray(L0_ffn_wu, np.float32)
                                .reshape(NE, 8, 128, NI))
    wd_m = np.ascontiguousarray(np.asarray(L0_ffn_wd, np.float32)
                                .reshape(NE, NIT, 128, 1024))
    wgs_m = np.ascontiguousarray(np.asarray(L0_ffn_wg_s, np.float32)
                                 .reshape(8, 128, NI))
    wus_m = np.ascontiguousarray(np.asarray(L0_ffn_wu_s, np.float32)
                                 .reshape(8, 128, NI))
    wds_m = np.ascontiguousarray(np.asarray(L0_ffn_wd_s, np.float32)
                                 .reshape(NIT, 128, 1024))
    ones_c = np.ones((128, 128), bf)
    ident_c = np.eye(128, dtype=np.float32)
    eps_c = np.full((128, 1), EPS, np.float32)
    onesfr_c = np.ones((128, 128), np.float32)
    sel_c = np.zeros((8, 1024), bf)
    for e in range(8):
        sel_c[e, e * 128:(e + 1) * 128] = 1
    in_maps = []
    for c in range(8):
        bb, h = c // 2, c % 2
        halves = [x[bb, 0:4096], x[bb, 4096:8192]]
        x_of = np.concatenate([halves[h], halves[1 - h]], 0)
        xTm = np.ascontiguousarray(x_of.T).reshape(8, 128, S)
        flag = np.full((128, 128), float(h), np.float32)
        in_maps.append({
            "xT": xTm, "WTm": WTm, "bqk": bqk, "bvb": bvb, "rT": rTm,
            "wg": wg_m, "wu": wu_m, "wd": wd_m, "wgs": wgs_m, "wus": wus_m,
            "wds": wds_m, "ones_in": ones_c, "ident_in": ident_c,
            "sel_in": sel_c, "flag_in": flag,
            "eps_in": eps_c, "onesfr_in": onesfr_c,
        })
    return in_maps


def kernel(**inputs):
    in_maps = _prep_inputs(**inputs)
    nc = build_program([[0, 1], [2, 3], [4, 5], [6, 7]])
    res = run_bass_kernel_spmd(nc, in_maps, core_ids=list(range(8)))
    x = np.asarray(inputs["L0_attn_x"], np.float32)
    y = np.empty_like(x)
    for c in range(8):
        bb, h = c // 2, c % 2
        o = np.asarray(res.results[c]["out"], np.float32).reshape(1024, TO)
        y[bb, h * 4096:(h + 1) * 4096, :] = o.T
    return y


# version 20 (final)
# speedup vs baseline: 1.1177x; 1.1177x over previous
import numpy as np
import ml_dtypes
import concourse.bass as bass
import concourse.mybir as mybir
from concourse.bass_utils import run_bass_kernel_spmd

F32 = mybir.dt.float32
BF16 = mybir.dt.bfloat16
AF = mybir.ActivationFunctionType
ALU = mybir.AluOpType
AX = mybir.AxisListType.X

S = 8192
TO = 4096
NE = 8
NI = 1408
NIT = 11
EPS = 1e-5


class Ctr:
    def __init__(self, handle):
        self.h = handle
        self.v = 0


class Q:

    def __init__(self, lanes):
        self.lanes = lanes
        self.i = 0

    def issue(self):
        lane = self.lanes[self.i % len(self.lanes)]
        self.i += 1
        lane.v += 16
        return (lane, lane.v)

    def barrier(self):
        return [(ln, ln.v) for ln in self.lanes if ln.v > 0]


class Emit:

    def __init__(self, sems):
        self.q = {k: [] for k in ("t", "v", "s", "y", "g")}
        self.selfsem = {"v": sems["vc"], "s": sems["sc"],
                        "g": sems.get("gc")}
        self.phase = "init" 

    def op(self, eng, fn, wait=(), inc=(), dma=False):
        waits = []
        for w in wait:
            if w is None:
                continue
            if isinstance(w, list):
                waits.extend(w)
            else:
                waits.append(w)
        waits = [(c, val) for (c, val) in waits if val > 0]
        chain = self.selfsem.get(eng)
        if chain is not None and not dma and chain.v > 0:
            waits.append((chain, chain.v))
        deltas = []
        for (c, n) in inc:
            c.v += n
            deltas.append((c, n))
        if (chain is not None and not dma
                and chain not in [c for (c, n) in deltas]):
            chain.v += 1
            deltas.append((chain, 1))

        def run(e):
            best = {}
            for (c, val) in waits:
                k = id(c)
                if k not in best or best[k][1] < val:
                    best[k] = (c, val)
            for (c, val) in best.values():
                e.wait_ge(c.h, val)
            ins = fn(e)
            if ins is not None:
                for (c, n) in deltas:
                    ins = ins.then_inc(c.h, n)
            return ins

        self.q[eng].append((self.phase, run))

    def dma(self, eng, q, fn, wait=()):
        tok = q.issue()
        self.op(eng, fn, wait=wait, inc=[tok[:1][0:0]] or [], dma=True)
        raise RuntimeError("use dma2")

    def dma2(self, eng, q, fn, wait=()):
        lane = q.lanes[q.i % len(q.lanes)]
        q.i += 1
        prev = (lane, lane.v) if lane.v > 0 else None
        lane.v += 16
        tok = (lane, lane.v)
        waits = []
        for w in wait:
            if w is None:
                continue
            if isinstance(w, list):
                waits.extend(w)
            else:
                waits.append(w)
        waits = [(c, val) for (c, val) in waits if val > 0]
        if prev is not None:
            waits.append(prev)

        def run(e):
            best = {}
            for (c, val) in waits:
                k = id(c)
                if k not in best or best[k][1] < val:
                    best[k] = (c, val)
            for (c, val) in best.values():
                e.wait_ge(c.h, val)
            ins = fn(e)
            ins.then_inc(lane.h, 16)
            return ins

        self.q[eng].append((self.phase, run))
        return tok


def build_program(replica_groups):
    nc = bass.Bass("TRN2", target_bir_lowering=False)

    xT = nc.dram_tensor("xT", [8, 128, S], F32, kind="ExternalInput").ap()
    WT = nc.dram_tensor("WTm", [8, 128, 3072], F32, kind="ExternalInput").ap()
    bqk = nc.dram_tensor("bqk", [128, 16], F32, kind="ExternalInput").ap()
    bvb = nc.dram_tensor("bvb", [128, 1024], BF16, kind="ExternalInput").ap()
    rT = nc.dram_tensor("rT", [128, 64], BF16, kind="ExternalInput").ap()
    wg = nc.dram_tensor("wg", [NE, 8, 128, NI], F32, kind="ExternalInput").ap()
    wu = nc.dram_tensor("wu", [NE, 8, 128, NI], F32, kind="ExternalInput").ap()
    wd = nc.dram_tensor("wd", [NE, NIT, 128, 1024], F32,
                        kind="ExternalInput").ap()
    wgs = nc.dram_tensor("wgs", [8, 128, NI], F32, kind="ExternalInput").ap()
    wus = nc.dram_tensor("wus", [8, 128, NI], F32, kind="ExternalInput").ap()
    wds = nc.dram_tensor("wds", [NIT, 128, 1024], F32,
                         kind="ExternalInput").ap()
    ones_in = nc.dram_tensor("ones_in", [128, 128], BF16,
                             kind="ExternalInput").ap()
    ident_in = nc.dram_tensor("ident_in", [128, 128], F32,
                              kind="ExternalInput").ap()
    sel_in = nc.dram_tensor("sel_in", [8, 1024], BF16,
                            kind="ExternalInput").ap()
    flag_in = nc.dram_tensor("flag_in", [128, 128], F32,
                             kind="ExternalInput").ap()
    eps_in = nc.dram_tensor("eps_in", [128, 1], F32, kind="ExternalInput").ap()
    onesfr_in = nc.dram_tensor("onesfr_in", [128, 128], mybir.dt.float32r,
                               kind="ExternalInput").ap()
    out = nc.dram_tensor("out", [8, 128, TO], F32, kind="ExternalOutput").ap()

    ln1_s = nc.dram_tensor("ln1_s", [8, 128, S], BF16).ap()
    kT_s = nc.dram_tensor("kT_s", [8, 128, S], BF16).ap()
    v_s = nc.dram_tensor("v_s", [64, 128, 1024], BF16).ap()
    E_s = nc.dram_tensor("E_s", [8, 64, 128, 512], BF16).ap()
    x2_s = nc.dram_tensor("x2_s", [8, 128, TO], F32).ap()
    hp_s = nc.dram_tensor("hp_s", [9, NIT, 128, TO], BF16).ap()
    wdb_s = nc.dram_tensor("wdb_s", [9, NIT, 128, 1024], BF16).ap()
    wgb_s = nc.dram_tensor("wgb_s", [9, 8, 128, NI], BF16).ap()
    wub_s = nc.dram_tensor("wub_s", [9, 8, 128, NI], BF16).ap()
    cc_in = nc.dram_tensor("cc_in", [128, 128], F32).ap()
    cc_out = nc.dram_tensor("cc_out", [256, 128], F32).ap()

    from contextlib import ExitStack
    es = ExitStack()
    with es:
        T1 = es.enter_context(nc.sbuf_tensor("T1", [128, 32768], BF16))
        T2 = es.enter_context(nc.sbuf_tensor("T2", [128, 32768], BF16))
        T3 = es.enter_context(nc.sbuf_tensor("T3", [128, 24576], BF16))
        T4 = es.enter_context(nc.sbuf_tensor("T4", [128, 15872], BF16))
        ps = es.enter_context(nc.psum_tensor("ps", [128, 4096], F32))

        F32R = mybir.dt.float32r

        def bank(i, w=512):
            return ps[:, i * 512: i * 512 + w]

        def b16(arena, off, n):
            return arena[:, off: off + n]

        def f32(arena, off, n):
            return arena[:].bitcast(F32)[:, off: off + n]

        ones_sb = b16(T4, 0, 128)
        onesfr_t = es.enter_context(nc.sbuf_tensor(
            "onesfr", [128, 128], mybir.dt.float32r))
        onesfr_sb = onesfr_t[:]
        ident_sb = f32(T4, 64, 128)
        flag_sb = f32(T4, 192, 128)
        bqk_sb = f32(T4, 320, 16)
        bvb_sb = b16(T4, 672, 1024)
        rT_sb = b16(T4, 1696, 64)
        m_all = f32(T4, 880, 64)
        z_all = f32(T4, 944, 64)
        zparts = f32(T4, 1008, 4)
        mtmp = f32(T4, 1012, 4)
        s0_sb = f32(T4, 1024, 128)
        s1_sb = f32(T4, 1152, 128)
        fj = f32(T4, 1280, 64)
        cm = f32(T4, 1344, 320)
        lg_sb = f32(T4, 1664, 256)
        cw_sb = f32(T4, 1920, 256)
        rtmp = f32(T4, 2176, 1088)
        eps_sb = f32(T4, 3263, 1)
        SEL = T4[0:8, 6528:7552]
        CWT = T4[0:8, 7552:11648]
        xbf2 = [b16(T4, 11648 + i * 1024, 1024) for i in range(2)]
        sq2 = [b16(T4, 13696 + i * 1024, 1024) for i in range(2)]
        d1_gs = [b16(T4, 11648, 2048), b16(T4, 13696, 2048)]

        a_xbf = [b16(T1, i * 1024, 1024) for i in range(8)]
        a_sq = [b16(T1, 8192 + i * 1024, 1024) for i in range(8)]
        a_xin = [f32(T2, 12288 + i * 2048, 2048) for i in range(2)]
        a_ln1 = [b16(T4, 6528, 2048), b16(T4, 8576, 2048),
                 b16(T4, 11648, 2048), b16(T4, 13696, 2048)]
        qT = T1[:]
        vpA = T1[:]
        xf = T1[:]

        mbb_sb = b16(T2, 0, 8192)
        isbb_sb = b16(T2, 8192, 8192)
        a3_ln1c = [b16(T2, 16384 + i * 4096, 4096) for i in range(2)]
        a_vs = [f32(T1, 8192 + i * 1024, 1024) for i in range(4)]
        a_mbf = [f32(T1, 12288 + i * 1024, 1024) for i in range(4)]
        a3_kbuf = [b16(T2, 8192 + i * 512, 512) for i in range(2)]
        a3_vbuf = [b16(T2, 9216 + i * 1024, 1024) for i in range(2)]
        eb = [b16(T2, i * 2048, 2048) for i in range(2)]
        kt_ring = [b16(T2, 8192 + i * 2048, 2048) for i in range(2)]
        vpB = T2[:]
        mb2_sb = f32(T2, 0, 4096)
        isb2_sb = f32(T2, 4096, 4096)
        mbb2_sb = b16(T2, 16384, 4096)
        isbb2_sb = b16(T2, 20480, 4096)
        wgu_cur = [b16(T2, 0, 11264), b16(T2, 11264, 11264)]
        cwb_sb = f32(T2, 11264, 4096)
        d2_hp = [b16(T2, 30720 + i * 512, 512) for i in range(4)]

        WT_sb = b16(T3, 0, 24576)
        e_ring = [b16(T3, i * 512, 512) for i in range(8)]
        cx_inb = [f32(T3, 2048, 4096), f32(T3, 2048, 4096)]
        cx2 = f32(T3, 6144, 4096)
        xin2 = [f32(T3, i * 2048, 2048) for i in range(2)]
        wgu_nxt = [b16(T3, 0, 11264), b16(T3, 11264, 11264)]
        d1_up = [b16(T3, 22528, 2048), b16(T4, 4352, 2048)]
        d2_wd = [b16(T3, i * 1024, 1024) for i in range(4)]
        d2_x2 = f32(T3, 2048, 4096)
        d2_x3 = f32(T3, 6144, 4096)

        sems = {}
        for name in ("tc", "vc", "sc", "cc", "gc"):
            sems[name] = Ctr(es.enter_context(nc.semaphore(name)))
        tc = sems["tc"]; vc = sems["vc"]; sc = sems["sc"]; cc = sems["cc"]
        gc = sems["gc"]

        def mk_q(pfx, n=8):
            return Q([Ctr(es.enter_context(nc.semaphore(f"{pfx}{i}")))
                      for i in range(n)])

        qA = mk_q("qA")
        qC = mk_q("qC")
        qO = mk_q("qO")
        qB = mk_q("qB")
        qW = mk_q("qW")
        qH = mk_q("qH", 12)
        qG = mk_q("qG", 1)
        qK = mk_q("qK")
        qO2 = mk_q("qO2")

        em = Emit(sems)
        E = em.op
        D = em.dma2

        k_consts = []
        k_consts.append(D("y", qC, lambda e: e.dma_start(out=ones_sb,
                                                         in_=ones_in)))
        k_consts.append(D("y", qC, lambda e: e.dma_start(out=ident_sb,
                                                         in_=ident_in)))
        k_consts.append(D("y", qC, lambda e: e.dma_start(out=flag_sb,
                                                         in_=flag_in)))
        k_consts.append(D("y", qC, lambda e: e.dma_start(out=bqk_sb,
                                                         in_=bqk)))
        k_consts.append(D("y", qC, lambda e: e.dma_start(out=bvb_sb,
                                                         in_=bvb)))
        k_consts.append(D("y", qC, lambda e: e.dma_start(out=rT_sb, in_=rT)))
        k_consts.append(D("y", qC, lambda e: e.dma_start(out=eps_sb,
                                                         in_=eps_in)))
        k_consts.append(D("y", qC, lambda e: e.dma_start(out=onesfr_sb,
                                                         in_=onesfr_in)))
        wt_tok = D("g", qG, lambda e: e.dma_start(
            out=WT_sb.rearrange("p (dt f) -> p dt f", dt=8),
            in_=WT.rearrange("dt p f -> p dt f")))

        em.phase = "a1"
        tmark = {}
        vmark = {}
        smark = {}
        F32R = mybir.dt.float32r
        t1addr = nc.lookup_mloc("T1").addr
        a1x = [nc.alloc_sbuf_tensor_at(
                   f"a1x{i}", [128, 4096], F32R,
                   offset=t1addr + i * 16384).ap() for i in range(2)]
        a1sq = [nc.alloc_sbuf_tensor_at(
                    f"a1sq{i}", [128, 1024], F32R,
                    offset=t1addr + 32768 + i * 4096).ap() for i in range(2)]
        a_vs = [f32(T1, 10240 + i * 1024, 1024) for i in range(2)]
        a_mbf = [f32(T1, 12288 + i * 1024, 1024) for i in range(2)]
        omark = {}
        a2x = [b16(T2, 24576 + i * 2048, 2048) for i in range(4)]

        def emit_a2_chunk(tcn):
            c0 = tcn * 2048
            for dt in range(8):
                g = tcn * 8 + dt
                xbt = a2x[g % 4]
                lo = a_ln1[g % 4]
                tok = D("g", qW, lambda e, dt=dt, c0=c0, xbt=xbt:
                    e.dma_start(out=xbt, in_=xT[dt, :, c0:c0 + 2048]),
                    wait=[vmark.get(("a2m", g - 4))])
                E("v", lambda e, xbt=xbt, lo=lo, c0=c0: e.tensor_tensor(
                    lo, xbt, mbb_sb[:, c0:c0 + 2048], ALU.subtract),
                    wait=[tok, smark[("a1i", tcn * 2)],
                          smark[("a1i", tcn * 2 + 1)],
                          omark.get(("a2", g - 4))])
                vmark[("a2m", g)] = (vc, vc.v)
                E("v", lambda e, lo=lo, c0=c0: e.tensor_tensor(
                    lo, lo, isbb_sb[:, c0:c0 + 2048], ALU.mult))
                omark[("a2", g)] = D("s", qO2, lambda e, dt=dt, c0=c0, lo=lo:
                    e.dma_start(out=ln1_s[dt, :, c0:c0 + 2048], in_=lo),
                    wait=[(vc, vc.v)])

        for scn in range(8):
            par = scn % 2
            c0 = scn * 1024
            for dg in range(2):
                gb = scn * 2 + dg
                xbuf = a1x[dg]
                ltoks = []
                for hf in range(2):
                    ltoks.append(D("y" if dg == 0 else "s",
                        qA if dg == 0 else qB,
                        lambda e, c0=c0, dg=dg, xbuf=xbuf, hf=hf:
                        e.dma_start(
                            out=xbuf[:, hf * 2048:hf * 2048 + 2048]
                            .rearrange("p (dt c) -> p dt c", dt=2),
                            in_=xT[dg * 4 + hf * 2:dg * 4 + hf * 2 + 2,
                                   :, c0:c0 + 1024]
                            .rearrange("dt p c -> p dt c").bitcast(F32R)),
                        wait=[tmark.get(("a1", gb - 2)),
                              vmark.get(("a1sq", gb - 2))]))
                for k in range(4):
                    dt = dg * 4 + k
                    ltok = ltoks[k // 2]
                    xsl = xbuf[:, k * 1024:k * 1024 + 1024]
                    sq = a1sq[dt % 2]
                    E("g", lambda e, xsl=xsl, sq=sq: e.tensor_tensor(
                        sq, xsl, xsl, ALU.mult),
                        wait=[ltok, tmark.get(("a1s", scn * 8 + dt - 2))])
                    sqv = (gc, gc.v)
                    for sub in range(2):
                        E("t", lambda e, par=par, sub=sub, xsl=xsl, dt=dt:
                            e.matmul(bank(par * 2 + sub), onesfr_sb,
                                     xsl[:, sub * 512:sub * 512 + 512],
                                     start=(dt == 0), stop=(dt == 7),
                                     skip_group_check=True),
                            wait=([ltok, k_consts] +
                                  ([vmark.get(("a1ev", scn - 2))]
                                   if dt == 0 else [])))
                    for sub in range(2):
                        E("t", lambda e, par=par, sub=sub, sq=sq, dt=dt:
                            e.matmul(bank(4 + par * 2 + sub), onesfr_sb,
                                     sq[:, sub * 512:sub * 512 + 512],
                                     start=(dt == 0), stop=(dt == 7),
                                     skip_group_check=True),
                            wait=[sqv] +
                                 ([vmark.get(("a1ev", scn - 2))]
                                  if dt == 0 else []),
                            inc=[(tc, 1)])
                    tmark[("a1s", scn * 8 + dt)] = (tc, tc.v)
                vmark[("a1sq", gb)] = (gc, gc.v)
                tmark[("a1", gb)] = (tc, tc.v)
            done = (tc, tc.v)
            vs = a_vs[scn % 2]
            mbf = a_mbf[scn % 2]
            E("v", lambda e, par=par, mbf=mbf: e.tensor_scalar(
                mbf, ps[:, par * 1024:par * 1024 + 1024],
                1.0 / 1024, None, ALU.mult),
                wait=[done, smark.get(("a1i", scn - 2))])
            E("v", lambda e, vs=vs, mbf=mbf: e.tensor_tensor(
                vs, mbf, mbf, ALU.mult))
            E("v", lambda e, par=par, vs=vs: e.scalar_tensor_tensor(
                vs, ps[:, 2048 + par * 1024:2048 + par * 1024 + 1024],
                1.0 / 1024, vs, ALU.mult, ALU.subtract))
            vmark[("a1ev", scn)] = (vc, vc.v)
            E("s", lambda e, c0=c0, mbf=mbf: e.copy(
                mbb_sb[:, c0:c0 + 1024], mbf), wait=[(vc, vc.v)])
            E("s", lambda e, vs=vs: e.activation(
                vs, vs, AF.Ln, bias=eps_sb))
            E("s", lambda e, c0=c0, vs=vs: e.activation(
                isbb_sb[:, c0:c0 + 1024], vs, AF.Exp, scale=-0.5))
            smark[("a1i", scn)] = (sc, sc.v)
            if scn % 2 == 1:
                emit_a2_chunk(scn // 2)

        em.phase = "a2"

        em.phase = "a3"
        last_kb = {}
        for tcn in range(16):
            c0 = tcn * 512
            lc = a3_ln1c[tcn % 2]
            ltok = D("g", qK, lambda e, c0=c0, lc=lc: e.dma_start(
                out=lc.rearrange("p (dt c) -> p dt c", dt=8),
                in_=ln1_s[:, :, c0:c0 + 512].rearrange("dt p c -> p dt c")),
                wait=[omark[("a2", (tcn // 4) * 8 + dd)] for dd in range(8)]
                + ([tmark.get(("a3", tcn - 2))] if tcn >= 2 else []))
            nft = 16 if tcn < 8 else 8
            for fi in range(nft):
                ftr = fi if tcn < 8 else 8 + fi
                bk = fi % 2
                for dt in range(8):
                    E("t", lambda e, bk=bk, dt=dt, ftr=ftr, lc=lc:
                        e.matmul(bank(bk),
                                 WT_sb[:, dt * 3072 + ftr * 128:
                                       dt * 3072 + ftr * 128 + 128],
                                 lc[:, dt * 512:dt * 512 + 512],
                                 start=(dt == 0), stop=(dt == 7),
                                 skip_group_check=True),
                        wait=([ltok, wt_tok,
                               (sc, smark[("a3qk", tcn, fi - 2)][1]
                                if fi >= 2 else sc.v)] +
                              ([vmark[("a1ev", 6)], vmark[("a1ev", 7)]]
                               if (tcn == 0 and fi < 2) else [])
                              if dt == 0 else []),
                        inc=([(tc, 1)] if dt == 7 else []))
                if ftr < 8:
                    E("s", lambda e, bk=bk, ftr=ftr, c0=c0: e.add(
                        qT[:, ftr * 4096 + c0: ftr * 4096 + c0 + 512],
                        bank(bk), bqk_sb[:, ftr:ftr + 1]),
                        wait=[(tc, tc.v)])
                else:
                    kb = a3_kbuf[fi % 2]
                    E("s", lambda e, bk=bk, ftr=ftr, kb=kb: e.add(
                        kb, bank(bk), bqk_sb[:, ftr:ftr + 1]),
                        wait=[(tc, tc.v), last_kb.get(fi % 2)])
                    last_kb[fi % 2] = D("y", qO,
                        lambda e, ftr=ftr, c0=c0, kb=kb: e.dma_start(
                            out=kT_s[ftr - 8, :, c0:c0 + 512], in_=kb),
                        wait=[(sc, sc.v)])
                smark[("a3qk", tcn, fi)] = (sc, sc.v)
            for tt4 in range(4):
                ttg = tcn * 4 + tt4
                vb = (2, 3) if tt4 % 2 == 0 else (4, 5)
                for kc in range(2):
                    for dt in range(8):
                        E("t", lambda e, vb=vb, kc=kc, dt=dt, tt4=tt4, lc=lc:
                            e.matmul(bank(vb[kc]),
                                     lc[:, dt * 512 + tt4 * 128:
                                        dt * 512 + tt4 * 128 + 128],
                                     WT_sb[:, dt * 3072 + 2048 + kc * 512:
                                           dt * 3072 + 2048 + kc * 512 + 512],
                                     start=(dt == 0), stop=(dt == 7),
                                     skip_group_check=True),
                            wait=([vmark.get(("a3v", ttg - 2))] +
                                  ([vmark[("a1ev", 6)], vmark[("a1ev", 7)]]
                                   if ttg < 2 else [])
                                  if (dt == 0 and kc == 0) else []),
                            inc=([(tc, 1)] if (dt == 7 and kc == 1) else []))
                vbuf = a3_vbuf[tt4 % 2]
                for kc in range(2):
                    E("v", lambda e, vb=vb, kc=kc, vbuf=vbuf: e.tensor_tensor(
                        vbuf[:, kc * 512:kc * 512 + 512], bank(vb[kc]),
                        bvb_sb[:, kc * 512:kc * 512 + 512], ALU.add),
                        wait=[(tc, tc.v), omark.get(("a3vs", ttg - 2))])
                vmark[("a3v", ttg)] = (vc, vc.v)
                omark[("a3vs", ttg)] = D("y", qO,
                    lambda e, ttg=ttg, vbuf=vbuf: e.dma_start(
                        out=v_s[ttg], in_=vbuf),
                    wait=[(vc, vc.v)])
            tmark[("a3", tcn)] = (tc, tc.v)
        a3_t = (tc, tc.v)
        a3_s = (sc, sc.v)
        a3_v = (vc, vc.v)
        a3_st = qO.barrier()

        em.phase = "b"
        negm_all = f32(T4, 2176, 64)
        zB_all = f32(T4, 2240, 64)
        ztmp = mtmp[:, 3:4]
        expm = {}
        bmark = {}
        for h in range(2):
            for jt in range(64):
                ring = kt_ring[(jt // 2) % 2]
                if jt % 2 == 0:
                    ktok = D("g", qK, lambda e, jt=jt, ring=ring: e.dma_start(
                        out=ring.rearrange("p (dt c) -> p dt c", dt=8),
                        in_=kT_s[:, :, jt * 128:jt * 128 + 256]
                        .rearrange("dt p c -> p dt c")),
                        wait=([a3_st] if (h == 0 and jt < 4)
                              else [bmark.get((h, jt - 3),
                                    bmark.get((h - 1, 61 + jt)))]))
                ebuf = eb[jt % 2]
                slot = (jt % 2) * 2048
                for c2 in range(2):
                    for dt in range(8):
                        for sub in range(2):
                            w = []
                            if dt == 0 and sub == 0 and c2 == 0:
                                w = [ktok, expm.get((h, jt - 2),
                                                    expm.get((h - 1, 62 + jt)))]
                                if h == 0 and jt < 2:
                                    w += [a3_s, a3_v]
                            E("t", lambda e, c2=c2, dt=dt, sub=sub, ring=ring,
                                slot=slot, h=h, jt=jt: e.matmul(
                                    ps[:, slot + c2 * 1024 + sub * 512:
                                       slot + c2 * 1024 + sub * 512 + 512],
                                    ring[:, dt * 256 + (jt % 2) * 128:
                                         dt * 256 + (jt % 2) * 128 + 128],
                                    qT[:, dt * 4096 + h * 2048 + c2 * 1024
                                       + sub * 512:
                                       dt * 4096 + h * 2048 + c2 * 1024
                                       + sub * 512 + 512],
                                    start=(dt == 0), stop=(dt == 7),
                                    skip_group_check=True),
                                wait=w,
                                inc=([(tc, 1)] if (dt == 7 and sub == 1)
                                     else []))
                mmdone = (tc, tc.v)
                bmark[(h, jt)] = mmdone
                if h == 0:
                    E("v", lambda e, slot=slot: e.tensor_reduce(
                        mtmp[:, 0:1], ps[:, slot:slot + 1024], AX, ALU.max),
                        wait=[mmdone, expm.get((0, jt - 2))])
                    E("v", lambda e, jt=jt: e.tensor_scalar(
                        negm_all[:, jt:jt + 1], mtmp[:, 0:1], -1.0 / 1024,
                        0.0, ALU.mult, ALU.min))
                    negm_v = (vc, vc.v)
                    zout = z_all[:, jt:jt + 1]
                else:
                    negm_v = None
                    zout = zB_all[:, jt:jt + 1]
                E("s", lambda e, ebuf=ebuf, slot=slot, jt=jt, zout=zout:
                    e.activation(
                        ebuf, ps[:, slot:slot + 2048],
                        AF.Exp, scale=1.0 / 1024,
                        bias=negm_all[:, jt:jt + 1], accum_out=zout),
                    wait=[mmdone, negm_v,
                          omark.get(("bE", h, jt - 2),
                                    omark.get(("bE", h - 1, 62 + jt)))] +
                         ([vmark.get(("bz", 0, jt - 1),
                                     vmark.get(("bz", -1, 63 + jt)))]
                          if h == 0 else []))
                expm[(h, jt)] = (sc, sc.v)
                if h == 0:
                    E("v", lambda e, jt=jt: e.tensor_scalar(
                        m_all[:, jt:jt + 1], negm_all[:, jt:jt + 1], -1.0,
                        None, ALU.mult), wait=[negm_v])
                    vmark[("bz", h, jt)] = (vc, vc.v)
                else:
                    pass
                omark[("bE", h, jt)] = D("y", qO,
                    lambda e, jt=jt, ebuf=ebuf, h=h: e.dma_start(
                        out=E_s[h * 4:h * 4 + 4, jt]
                        .rearrange("ic p c -> p ic c"),
                        in_=ebuf.rearrange("p (ic c) -> p ic c", ic=4)),
                    wait=[expm[(h, jt)]])
        E("v", lambda e: e.tensor_tensor(z_all, z_all, zB_all, ALU.add),
            wait=[(sc, sc.v)])
        b_t = (tc, tc.v)
        b_s = (sc, sc.v)
        b_v = (vc, vc.v)

        em.phase = "x"
        t1 = D("y", qO, lambda e: e.dma_start(out=cc_in[:, 0:64], in_=m_all),
               wait=[b_v])
        t2 = D("y", qO, lambda e: e.dma_start(out=cc_in[:, 64:128],
                                              in_=z_all), wait=[b_v])
        E("g", lambda e: e.collective_compute(
            "AllGather", ALU.bypass, ins=[cc_in], outs=[cc_out],
            replica_groups=replica_groups),
            wait=[t1, t2], inc=[(cc, 1)], dma=True)
        g1 = D("g", qH, lambda e: e.dma_start(out=s0_sb,
                                              in_=cc_out[0:128, :]),
               wait=[(cc, cc.v)])
        g2 = D("g", qH, lambda e: e.dma_start(out=s1_sb,
                                              in_=cc_out[128:256, :]),
               wait=[(cc, cc.v)])
        E("v", lambda e: e.tensor_tensor(
            cm[:, 0:128], s0_sb, flag_sb, ALU.mult), wait=[g1, g2])
        E("v", lambda e: e.tensor_scalar(
            cm[:, 128:256], flag_sb, -1.0, 1.0, ALU.mult, ALU.add))
        E("v", lambda e: e.tensor_tensor(
            cm[:, 128:256], cm[:, 128:256], s1_sb, ALU.mult))
        E("v", lambda e: e.tensor_tensor(
            cm[:, 0:128], cm[:, 0:128], cm[:, 128:256], ALU.add))
        for half in range(2):
            my = m_all[:, half * 32:half * 32 + 32]
            pa = cm[:, (1 - half) * 32:(1 - half) * 32 + 32]
            E("v", lambda e, my=my, pa=pa, half=half: e.tensor_tensor(
                cm[:, 128 + half * 32:128 + half * 32 + 32], my, pa,
                ALU.max))
        mg = cm[:, 128:192]
        E("v", lambda e: e.tensor_tensor(
            cm[:, 192:256], m_all, mg, ALU.subtract))
        for half in range(2):
            pa = cm[:, (1 - half) * 32:(1 - half) * 32 + 32]
            E("v", lambda e, pa=pa, half=half: e.tensor_tensor(
                cm[:, 256 + half * 32:256 + half * 32 + 32], pa,
                mg[:, half * 32:half * 32 + 32], ALU.subtract))
        E("s", lambda e: e.activation(cm[:, 192:256], cm[:, 192:256],
                                      AF.Exp), wait=[(vc, vc.v)])
        E("s", lambda e: e.activation(cm[:, 256:320], cm[:, 256:320],
                                      AF.Exp))
        e01 = (sc, sc.v)
        E("v", lambda e: e.tensor_tensor(
            mg, z_all, cm[:, 192:256], ALU.mult), wait=[e01])
        for half in range(2):
            pz_ = cm[:, 64 + (1 - half) * 32:64 + (1 - half) * 32 + 32]
            E("v", lambda e, pz_=pz_, half=half: e.tensor_tensor(
                cm[:, 256 + half * 32:256 + half * 32 + 32], pz_,
                cm[:, 256 + half * 32:256 + half * 32 + 32], ALU.mult))
        E("v", lambda e: e.tensor_tensor(
            mg, mg, cm[:, 256:320], ALU.add))
        E("v", lambda e: e.reciprocal(cm[:, 256:320], mg))
        E("v", lambda e: e.tensor_tensor(
            fj, cm[:, 192:256], cm[:, 256:320], ALU.mult))
        fj_v = (vc, vc.v)

        em.phase = "c"
        vp_v = (vc, vc.v)
        evD = {k: None for k in range(8)}
        for ic in range(8):
            cx_in = cx_inb[ic % 2]
            cxtok = D("y", qA, lambda e, ic=ic, cx_in=cx_in: e.dma_start(
                out=cx_in.rearrange("p (kt c) -> p kt c", kt=8),
                in_=xT[:, :, ic * 512:ic * 512 + 512]
                .rearrange("kt p c -> p kt c")),
                wait=[vmark.get(("cev", ic - 1))])
            for jt in range(64):
                if ic == 0:
                    arena = vpA if jt < 32 else vpB
                    vt = arena[:, (jt % 32) * 1024:(jt % 32) * 1024 + 1024]
                    vtok = D("y", qA, lambda e, jt=jt, vt=vt: e.dma_start(
                        out=vt, in_=v_s[jt]),
                        wait=[fj_v, b_t, omark[("a3vs", jt)],
                              vmark.get(("vp", jt - 8))] +
                             ([omark[("bE", 1, 62)], omark[("bE", 1, 63)]]
                              if jt >= 32 else []))
                    E("s", lambda e, jt=jt, vt=vt: e.mul(
                        vt, vt, fj[:, jt:jt + 1]),
                        wait=[vtok, fj_v])
                    vmark[("vp", jt)] = (sc, sc.v)
                er = e_ring[jt % 8]
                etok = D("s", qB, lambda e, ic=ic, jt=jt, er=er: e.dma_start(
                    out=er, in_=E_s[ic, jt]),
                    wait=[omark[("bE", ic // 4, jt)] if ic in (0, 4)
                          else None,
                          tmark.get(("c", ic, jt - 8),
                                    tmark.get(("c", ic - 1, 56 + jt)))])
                arena = vpA if jt < 32 else vpB
                col = (jt % 32) * 1024
                for kt in range(8):
                    w = []
                    if kt == 0:
                        w = [etok,
                             vmark[("vp", jt)] if ic == 0 else None]
                    if jt == 0:
                        w = w + [evD[kt]] + ([b_s] if ic == 0 else [])
                    E("t", lambda e, kt=kt, er=er, arena=arena, col=col,
                        jt=jt: e.matmul(
                            bank(kt),
                            arena[:, col + kt * 128: col + kt * 128 + 128],
                            er, start=(jt == 0), stop=(jt == 63),
                            skip_group_check=True),
                        wait=w,
                        inc=[(tc, 1)] if (kt == 7 or jt == 63) else [])
                tmark[("c", ic, jt)] = (tc, tc.v)
            tl = (tc, tc.v)
            E("v", lambda e, cx_in=cx_in: e.tensor_tensor(
                cx2[:, 0:2048], ps[:, 0:2048], cx_in[:, 0:2048], ALU.add),
                wait=[(tc, tl[1] - 4), cxtok, omark.get(("cx2", ic - 1))])
            for kt in range(4):
                evD[kt] = (vc, vc.v)
            E("v", lambda e, cx_in=cx_in: e.tensor_tensor(
                cx2[:, 2048:4096], ps[:, 2048:4096], cx_in[:, 2048:4096],
                ALU.add), wait=[tl])
            for kt in range(4, 8):
                evD[kt] = (vc, vc.v)
            vmark[("cev", ic)] = (vc, vc.v)
            omark[("cx2", ic)] = D("y", qO, lambda e, ic=ic: e.dma_start(
                out=x2_s[:, :, ic * 512:ic * 512 + 512]
                .rearrange("dt p c -> p dt c"),
                in_=cx2.rearrange("p (kt c) -> p kt c", kt=8)),
                wait=[(vc, vc.v)])
        c_st = qO.barrier()

        em.phase = "d"
        sel_tok = D("y", qC, lambda e: e.dma_start(out=SEL, in_=sel_in),
                    wait=[omark[("a2", 28 + dd)] for dd in range(4)])
        for scn in range(4):
            par = scn % 2
            c0 = scn * 1024
            for dt in range(8):
                g = scn * 8 + dt
                xb = xbf2[g % 2]
                sq = sq2[g % 2]
                ldt = D("g", qW, lambda e, dt=dt, c0=c0, xb=xb: e.dma_start(
                    out=xb, in_=x2_s[dt, :, c0:c0 + 1024]),
                    wait=[omark[("cx2", scn * 2)],
                          omark[("cx2", scn * 2 + 1)],
                          tmark.get(("d", g - 2)),
                          vmark.get(("dsq", g - 2))])
                E("v", lambda e, xb=xb, sq=sq: e.tensor_tensor(
                    sq, xb, xb, ALU.mult), wait=[ldt])
                vmark[("dsq", g)] = (vc, vc.v)
                for sub in range(2):
                    E("t", lambda e, par=par, sub=sub, xb=xb, dt=dt:
                        e.matmul(bank(par * 2 + sub), ones_sb,
                                 xb[:, sub * 512:sub * 512 + 512],
                                 start=(dt == 0), stop=(dt == 7),
                                 skip_group_check=True),
                        wait=[ldt] +
                             ([evD[par * 2 + sub],
                               vmark.get(("dev", scn - 2))]
                              if dt == 0 else []))
                for sub in range(2):
                    E("t", lambda e, par=par, sub=sub, sq=sq, dt=dt:
                        e.matmul(bank(4 + par * 2 + sub), ones_sb,
                                 sq[:, sub * 512:sub * 512 + 512],
                                 start=(dt == 0), stop=(dt == 7),
                                 skip_group_check=True),
                        wait=[vmark[("dsq", g)]] +
                             ([evD[4 + par * 2 + sub],
                               vmark.get(("dev", scn - 2))]
                              if dt == 0 else []),
                        inc=[(tc, 1)])
                tmark[("d", g)] = (tc, tc.v)
            done = (tc, tc.v)
            E("v", lambda e, par=par, c0=c0: e.tensor_scalar(
                mb2_sb[:, c0:c0 + 1024], ps[:, par * 1024:par * 1024 + 1024],
                1.0 / 1024, None, ALU.mult), wait=[done, vp_v])
            E("v", lambda e, c0=c0: e.tensor_tensor(
                isb2_sb[:, c0:c0 + 1024], mb2_sb[:, c0:c0 + 1024],
                mb2_sb[:, c0:c0 + 1024], ALU.mult))
            E("v", lambda e, par=par, c0=c0: e.scalar_tensor_tensor(
                isb2_sb[:, c0:c0 + 1024],
                ps[:, 2048 + par * 1024:2048 + par * 1024 + 1024],
                1.0 / 1024, isb2_sb[:, c0:c0 + 1024], ALU.mult,
                ALU.subtract))
            vmark[("dev", scn)] = (vc, vc.v)
            E("s", lambda e, c0=c0: e.activation(
                isb2_sb[:, c0:c0 + 1024], isb2_sb[:, c0:c0 + 1024], AF.Ln,
                bias=eps_sb), wait=[(vc, vc.v)])
            E("s", lambda e, c0=c0: e.activation(
                isb2_sb[:, c0:c0 + 1024], isb2_sb[:, c0:c0 + 1024], AF.Exp,
                scale=-0.5))
        ln2_s_m = (sc, sc.v)

        for q4 in range(4):
            E("v", lambda e, q4=q4: e.tensor_scalar(
                mbb2_sb[:, q4 * 1024:q4 * 1024 + 1024],
                mb2_sb[:, q4 * 1024:q4 * 1024 + 1024], 1.0, None, ALU.mult),
                wait=[(sc, ln2_s_m[1])] if q4 == 0 else [])
            E("v", lambda e, q4=q4: e.tensor_scalar(
                isbb2_sb[:, q4 * 1024:q4 * 1024 + 1024],
                isb2_sb[:, q4 * 1024:q4 * 1024 + 1024], 1.0, None, ALU.mult))
        cvt_v = (vc, vc.v)
        xb2l = [b16(T3, i * 2048, 2048) for i in range(4)]
        for tn in range(2):
            for dt in range(8):
                g = tn * 8 + dt
                c0 = tn * 2048
                xbt = xb2l[g % 4]
                xtok = D("g", qW, lambda e, dt=dt, c0=c0, xbt=xbt:
                    e.dma_start(out=xbt, in_=x2_s[dt, :, c0:c0 + 2048]),
                    wait=[vmark.get(("dap", g - 4)), (tc, tc.v),
                          vmark[("cev", 7)]])
                E("v", lambda e, xbt=xbt, c0=c0: e.tensor_tensor(
                    xbt, xbt, mbb2_sb[:, c0:c0 + 2048], ALU.subtract),
                    wait=[xtok, cvt_v])
                E("v", lambda e, xbt=xbt, dt=dt, c0=c0: e.tensor_tensor(
                    xf[:, dt * 4096 + c0:dt * 4096 + c0 + 2048], xbt,
                    isbb2_sb[:, c0:c0 + 2048], ALU.mult))
                vmark[("dap", g)] = (vc, vc.v)
            vmark[("dapH", tn)] = (vc, vc.v)
        xf_v = (vc, vc.v)

        for tt in range(32):
            bk = tt % 8
            for dt in range(8):
                E("t", lambda e, tt=tt, dt=dt, bk=bk: e.matmul(
                    ps[:, bk * 512:bk * 512 + 8],
                    xf[:, dt * 4096 + tt * 128:dt * 4096 + tt * 128 + 128],
                    rT_sb[:, dt * 8:dt * 8 + 8],
                    start=(dt == 0), stop=(dt == 7), skip_group_check=True),
                    wait=([vmark[("dapH", tt // 16)],
                           vmark.get(("rt", tt - 8))]
                          if dt == 0 else []),
                    inc=([(tc, 1)] if dt == 7 else []))
            E("v", lambda e, tt=tt, bk=bk: e.tensor_copy(
                lg_sb[:, tt * 8:tt * 8 + 8], ps[:, bk * 512:bk * 512 + 8]),
                wait=[(tc, tc.v)])
            vmark[("rt", tt)] = (vc, vc.v)
        lg_v = (vc, vc.v)

        lg3 = lg_sb.rearrange("p (t e) -> p t e", e=8)
        ge1 = rtmp[:, 0:256]
        pzt = rtmp[:, 256:512]
        ge2 = rtmp[:, 512:768]
        m2c = rtmp[:, 768:800]
        denc = rtmp[:, 800:832]
        exl = rtmp[:, 832:1088]
        mxc = cm[:, 0:32]
        E("v", lambda e: e.tensor_reduce(mxc.unsqueeze(2), lg3, AX, ALU.max),
            wait=[lg_v])
        E("v", lambda e: e.tensor_tensor(
            ge1.rearrange("p (t e) -> p t e", e=8), lg3,
            mxc.unsqueeze(2).broadcast_to([128, 32, 8]), ALU.is_ge))
        E("v", lambda e: e.tensor_tensor(
            exl.rearrange("p (t e) -> p t e", e=8), lg3,
            mxc.unsqueeze(2).broadcast_to([128, 32, 8]), ALU.subtract))
        E("s", lambda e: e.activation(exl, exl, AF.Exp), wait=[(vc, vc.v)])
        ex_s = (sc, sc.v)
        E("v", lambda e: e.scalar_tensor_tensor(
            pzt, ge1, -1.0e30, lg_sb, ALU.mult, ALU.add))
        E("v", lambda e: e.tensor_reduce(
            m2c.unsqueeze(2), pzt.rearrange("p (t e) -> p t e", e=8), AX,
            ALU.max))
        E("v", lambda e: e.tensor_tensor(
            ge2.rearrange("p (t e) -> p t e", e=8), lg3,
            m2c.unsqueeze(2).broadcast_to([128, 32, 8]), ALU.is_ge))
        E("v", lambda e: e.tensor_tensor(cw_sb, exl, ge2, ALU.mult),
            wait=[ex_s])
        E("v", lambda e: e.tensor_reduce(
            denc.unsqueeze(2), cw_sb.rearrange("p (t e) -> p t e", e=8), AX,
            ALU.add))
        E("v", lambda e: e.reciprocal(m2c, denc))
        E("v", lambda e: e.tensor_tensor(
            cw_sb.rearrange("p (t e) -> p t e", e=8),
            cw_sb.rearrange("p (t e) -> p t e", e=8),
            m2c.unsqueeze(2).broadcast_to([128, 32, 8]), ALU.mult))
        cw_v = (vc, vc.v)

        for tt in range(32):
            bkq = (tt % 4) * 512
            E("t", lambda e, tt=tt, bkq=bkq: e.transpose(
                ps[0:8, bkq:bkq + 128], cw_sb[:, tt * 8:tt * 8 + 8],
                ident_sb),
                wait=[cw_v if tt < 4 else vmark[("ct", tt - 4)]],
                inc=[(tc, 1)])
            E("v", lambda e, tt=tt, bkq=bkq: e.tensor_copy(
                CWT[:, tt * 128:tt * 128 + 128], ps[0:8, bkq:bkq + 128]),
                wait=[(tc, tc.v)])
            vmark[("ct", tt)] = (vc, vc.v)
        cwt_v = (vc, vc.v)

        em.phase = "d1"
        def wg_ap(e_):
            return wg[e_] if e_ < 8 else wgs

        def wu_ap(e_):
            return wu[e_] if e_ < 8 else wus

        def wd_ap(e_):
            return wd[e_] if e_ < 8 else wds

        wld = {}
        t1 = D("g", qW, lambda e: e.dma_start(
            out=wgu_cur[0].rearrange("p (ht i) -> p ht i", ht=8),
            in_=wg[0].rearrange("ht p i -> p ht i")),
            wait=[xf_v, c_st])
        t2 = D("g", qW, lambda e: e.dma_start(
            out=wgu_cur[1].rearrange("p (ht i) -> p ht i", ht=8),
            in_=wu[0].rearrange("ht p i -> p ht i")))
        wld[0] = [t1, t2]
        d1s = (sc, sc.v)
        d1v = (vc, vc.v)
        hpm = {}
        for en in range(9):
            cur = wgu_cur if en % 2 == 0 else wgu_nxt
            nxt = wgu_nxt if en % 2 == 0 else wgu_cur
            if en < 8:
                t1 = D("g", qW, lambda e, en=en, nxt=nxt: e.dma_start(
                    out=nxt[0].rearrange("p (ht i) -> p ht i", ht=8),
                    in_=wg_ap(en + 1).rearrange("ht p i -> p ht i")),
                    wait=[(tc, tc.v)] + ([xf_v] if en == 0 else []))
                t2 = D("g", qW, lambda e, en=en, nxt=nxt: e.dma_start(
                    out=nxt[1].rearrange("p (ht i) -> p ht i", ht=8),
                    in_=wu_ap(en + 1).rearrange("ht p i -> p ht i")))
                wld[en + 1] = [t1, t2]
            hpm[("wd", en)] = D("g", qH, lambda e, en=en: e.dma_start(
                out=wdb_s[en], in_=wd_ap(en)))
            if en < 8:
                for ch in range(8):
                    E("t", lambda e, en=en, ch=ch: e.matmul(
                        bank(ch % 4), SEL[:, en * 128:en * 128 + 128],
                        CWT[:, ch * 512:ch * 512 + 512],
                        start=True, stop=True, skip_group_check=True),
                        wait=[cwt_v, d1s, sel_tok,
                              vmark.get(("cwb", en, ch - 4))],
                        inc=[(tc, 1)])
                    E("v", lambda e, ch=ch: e.tensor_copy(
                        cwb_sb[:, ch * 512:ch * 512 + 512], bank(ch % 4)),
                        wait=[(tc, tc.v)])
                    vmark[("cwb", en, ch)] = (vc, vc.v)
            cwb_v = (vc, vc.v)
            for it in range(NIT):
                for th in range(2):
                    g = (en * NIT + it) * 2 + th
                    par = g % 2
                    gs_t = d1_gs[par]
                    up_t = d1_up[par]
                    for ht in range(8):
                        for sub in range(4):
                            w = []
                            if ht == 0 and sub == 0:
                                w = wld[en] + [cwb_v, d1s, xf_v]
                            E("t", lambda e, cur=cur, ht=ht, sub=sub, it=it,
                                th=th: e.matmul(
                                    bank(sub),
                                    cur[0][:, ht * NI + it * 128:
                                           ht * NI + it * 128 + 128],
                                    xf[:, ht * 4096 + th * 2048 + sub * 512:
                                       ht * 4096 + th * 2048 + sub * 512 + 512],
                                    start=(ht == 0), stop=(ht == 7),
                                    skip_group_check=True),
                                wait=w,
                                inc=([(tc, 1)] if (ht == 7 and sub == 3)
                                     else []))
                    g_t = (tc, tc.v)
                    for ht in range(8):
                        for sub in range(4):
                            w = []
                            if ht == 0 and sub == 0:
                                w = [d1v]
                            E("t", lambda e, cur=cur, ht=ht, sub=sub, it=it,
                                th=th: e.matmul(
                                    bank(4 + sub),
                                    cur[1][:, ht * NI + it * 128:
                                           ht * NI + it * 128 + 128],
                                    xf[:, ht * 4096 + th * 2048 + sub * 512:
                                       ht * 4096 + th * 2048 + sub * 512 + 512],
                                    start=(ht == 0), stop=(ht == 7),
                                    skip_group_check=True),
                                wait=w,
                                inc=([(tc, 1)] if (ht == 7 and sub == 3)
                                     else []))
                    u_t = (tc, tc.v)
                    E("s", lambda e, gs_t=gs_t: e.activation(
                        gs_t, ps[:, 0:2048], AF.Silu),
                        wait=[g_t, vmark.get(("d1h", g - 2))])
                    d1s = (sc, sc.v)
                    if en < 8:
                        E("v", lambda e, up_t=up_t, th=th: e.tensor_tensor(
                            up_t, ps[:, 2048:4096],
                            cwb_sb[:, th * 2048:th * 2048 + 2048], ALU.mult),
                            wait=[u_t, hpm.get(("hp", g - 2))])
                    else:
                        E("v", lambda e, up_t=up_t: e.tensor_scalar(
                            up_t, ps[:, 2048:4096], 1.0, None, ALU.mult),
                            wait=[u_t, hpm.get(("hp", g - 2))])
                    d1v = (vc, vc.v)
                    E("v", lambda e, gs_t=gs_t, up_t=up_t: e.tensor_tensor(
                        up_t, gs_t, up_t, ALU.mult),
                        wait=[d1s])
                    vmark[("d1h", g)] = (vc, vc.v)
                    hpm[("hp", g)] = D("y", qO,
                        lambda e, en=en, it=it, th=th, up_t=up_t:
                        e.dma_start(
                            out=hp_s[en, it, :, th * 2048:th * 2048 + 2048],
                            in_=up_t),
                        wait=[(vc, vc.v)])
        d1_t = (tc, tc.v)
        d1_fs = (sc, sc.v)
        d1_fv = (vc, vc.v)

        em.phase = "d2"
        d2_x2b = f32(T1, 0, 4096)
        d2_x3b = f32(T1, 4096, 4096)
        hrF = [b16(T2, 0, 5632), b16(T2, 5632, 5632)]
        wdF = [b16(T3, 0, 11264), b16(T3, 11264, 11264)]
        wdR = [b16(T1, 16384 + i * 11264, 11264) for i in range(1)] + \
              [b16(T2, 11264 + i * 11264, 11264) for i in range(1)] + \
              [b16(T4, 6528, 11264 - 2048)] + []
        wdR = wdR[:2]
        wdr_tok = []
        for i in range(2):
            wdr_tok.append(D("s", qB, lambda e, i=i, w=wdR[i]: e.dma_start(
                out=w.rearrange("p (it c) -> p it c", it=NIT),
                in_=wdb_s[i].rearrange("it p c -> p it c")),
                wait=[hpm[("wd", i)], d1_t]))
        yev = {k: None for k in range(8)}
        d2m = {}
        for tn in range(8):
            c0 = tn * 512
            x2tok = D("y", qA, lambda e, c0=c0: e.dma_start(
                out=d2_x2b.rearrange("p (kt c) -> p kt c", kt=8),
                in_=x2_s[:, :, c0:c0 + 512].rearrange("dt p c -> p dt c")),
                wait=[vmark.get(("d2ev", tn - 1)), d1_t if tn == 0 else None])
            for en in range(9):
                gg = tn * 9 + en
                par = gg % 2
                hr = hrF[par]
                wr = wdF[par]
                hw = [hpm[("hp", (en * NIT + i2) * 2 + t2)]
                      for i2 in range(NIT) for t2 in range(2)]
                htok = D("s", qB, lambda e, en=en, c0=c0, hr=hr: e.dma_start(
                    out=hr.rearrange("p (it c) -> p it c", it=NIT),
                    in_=hp_s[en, :, :, c0:c0 + 512]
                    .rearrange("it p c -> p it c")),
                    wait=hw + [d2m.get(gg - 2),
                               d1_t if gg < 2 else None])
                if en < 2:
                    wr = wdR[en]
                    wtok = wdr_tok[en]
                else:
                    wtok = D("s", qB, lambda e, en=en, wr=wr: e.dma_start(
                        out=wr.rearrange("p (it c) -> p it c", it=NIT),
                        in_=wdb_s[en].rearrange("it p c -> p it c")),
                        wait=[hpm[("wd", en)], d2m.get(gg - 2),
                              d1_t if gg < 2 else None])
                first = (en == 0)
                last = (en == 8)
                for it in range(NIT):
                    for ht in range(8):
                        w = []
                        if it == 0 and ht == 0:
                            w = [htok, wtok]
                        if first and it == 0:
                            w = w + [yev[ht]] + \
                                ([d1_fs, d1_fv] if tn == 0 else [])
                        E("t", lambda e, it=it, ht=ht, hr=hr, wr=wr,
                            first=first, last=last: e.matmul(
                                bank(ht),
                                wr[:, it * 1024 + ht * 128:
                                   it * 1024 + ht * 128 + 128],
                                hr[:, it * 512:it * 512 + 512],
                                start=(first and it == 0),
                                stop=(last and it == NIT - 1),
                                skip_group_check=True),
                            wait=w,
                            inc=([(tc, 1)] if (ht == 7 and it == NIT - 1)
                                 else []))
                d2m[gg] = (tc, tc.v)
            tl = (tc, tc.v)
            for ht in range(8):
                E("v", lambda e, ht=ht: e.tensor_tensor(
                    d2_x3b[:, ht * 512:ht * 512 + 512], bank(ht),
                    d2_x2b[:, ht * 512:ht * 512 + 512], ALU.add),
                    wait=[tl, x2tok, omark.get(("out", tn - 1))])
                yev[ht] = (vc, vc.v)
            vmark[("d2ev", tn)] = (vc, vc.v)
            omark[("out", tn)] = D("y", qO, lambda e, c0=c0: e.dma_start(
                out=out[:, :, c0:c0 + 512].rearrange("dt p c -> p dt c"),
                in_=d2_x3b.rearrange("p (kt c) -> p kt c", kt=8)),
                wait=[(vc, vc.v)])
        fin = qO.barrier()

        E("g", lambda e: [e.wait_ge(c.h, v) for (c, v) in fin][-1], dma=True)

        def replay(e, key):
            items = em.q[key]
            i = 0
            while i < len(items):
                ph = items[i][0]
                j = i
                with nc.named_scope(f"{ph}_{key}"):
                    while j < len(items) and items[j][0] == ph:
                        items[j][1](e)
                        j += 1
                i = j

        with nc.Block() as block:
            @block.sync
            def _(e):
                replay(e, "y")

            @block.tensor
            def _(e):
                replay(e, "t")

            @block.vector
            def _(e):
                replay(e, "v")

            @block.scalar
            def _(e):
                replay(e, "s")

            @block.gpsimd
            def _(e):
                replay(e, "g")

    return nc


def _prep_inputs(L0_attn_x, L0_attn_W, L0_attn_b, L0_ffn_w_router, L0_ffn_wg,
                 L0_ffn_wu, L0_ffn_wd, L0_ffn_wg_s, L0_ffn_wu_s, L0_ffn_wd_s):
    bf = ml_dtypes.bfloat16
    x = np.asarray(L0_attn_x, np.float32)
    W = np.asarray(L0_attn_W, np.float32)
    b = np.asarray(L0_attn_b, np.float32)
    WTm = np.ascontiguousarray(W.T).reshape(8, 128, 3072)
    bqk = np.ascontiguousarray(b[0:2048].reshape(16, 128).T)
    bvb = np.ascontiguousarray(np.tile(b[2048:3072], (128, 1))).astype(bf)
    rTm = np.ascontiguousarray(
        np.asarray(L0_ffn_w_router, np.float32).T.reshape(8, 128, 8)
        .transpose(1, 0, 2).reshape(128, 64)).astype(bf)
    wg_m = np.ascontiguousarray(np.asarray(L0_ffn_wg, np.float32)
                                .reshape(NE, 8, 128, NI))
    wu_m = np.ascontiguousarray(np.asarray(L0_ffn_wu, np.float32)
                                .reshape(NE, 8, 128, NI))
    wd_m = np.ascontiguousarray(np.asarray(L0_ffn_wd, np.float32)
                                .reshape(NE, NIT, 128, 1024))
    wgs_m = np.ascontiguousarray(np.asarray(L0_ffn_wg_s, np.float32)
                                 .reshape(8, 128, NI))
    wus_m = np.ascontiguousarray(np.asarray(L0_ffn_wu_s, np.float32)
                                 .reshape(8, 128, NI))
    wds_m = np.ascontiguousarray(np.asarray(L0_ffn_wd_s, np.float32)
                                 .reshape(NIT, 128, 1024))
    ones_c = np.ones((128, 128), bf)
    ident_c = np.eye(128, dtype=np.float32)
    eps_c = np.full((128, 1), EPS, np.float32)
    onesfr_c = np.ones((128, 128), np.float32)
    sel_c = np.zeros((8, 1024), bf)
    for e in range(8):
        sel_c[e, e * 128:(e + 1) * 128] = 1
    in_maps = []
    for c in range(8):
        bb, h = c // 2, c % 2
        halves = [x[bb, 0:4096], x[bb, 4096:8192]]
        x_of = np.concatenate([halves[h], halves[1 - h]], 0)
        xTm = np.ascontiguousarray(x_of.T).reshape(8, 128, S)
        flag = np.full((128, 128), float(h), np.float32)
        in_maps.append({
            "xT": xTm, "WTm": WTm, "bqk": bqk, "bvb": bvb, "rT": rTm,
            "wg": wg_m, "wu": wu_m, "wd": wd_m, "wgs": wgs_m, "wus": wus_m,
            "wds": wds_m, "ones_in": ones_c, "ident_in": ident_c,
            "sel_in": sel_c, "flag_in": flag,
            "eps_in": eps_c, "onesfr_in": onesfr_c,
        })
    return in_maps


def kernel(**inputs):
    in_maps = _prep_inputs(**inputs)
    nc = build_program([[0, 1], [2, 3], [4, 5], [6, 7]])
    res = run_bass_kernel_spmd(nc, in_maps, core_ids=list(range(8)))
    x = np.asarray(inputs["L0_attn_x"], np.float32)
    y = np.empty_like(x)
    for c in range(8):
        bb, h = c // 2, c % 2
        o = np.asarray(res.results[c]["out"], np.float32).reshape(1024, TO)
        y[bb, h * 4096:(h + 1) * 4096, :] = o.T
    return y
